# version 26
# baseline (speedup 1.0000x reference)
import numpy as np
import ml_dtypes
import concourse.bass as bass
import concourse.mybir as mybir
from concourse.bass_utils import run_bass_kernel_spmd

F32 = mybir.dt.float32
BF16 = mybir.dt.bfloat16
U32 = mybir.dt.uint32
AF = mybir.ActivationFunctionType
ALU = mybir.AluOpType
AX = mybir.AxisListType
IOX = bass.IndirectOffsetOnAxis

S, D, NH, HD = 2048, 1024, 16, 64
HP = NH // 2
TC = S // 128
E, FH, CAP = 32, 256, 128
PERM_ROWS = E * CAP + S
NSLOT = 8
NEARLY = 1
EXP_SHIFT = -4.0
ATTN_SCALE = 0.125
LN_EPS = 1e-5
NEG_BIG = -1e30
DEBUG_DUMPS = False


class Ctr:
    def __init__(self, step):
        self.step = step
        self.v = 0


class Builder:
    def __init__(self, nc):
        self.nc = nc
        self.lists = {e: [] for e in ("sync", "scalar", "vector", "tensor", "gpsimd")}
        self.sems = {}
        self.ctrs = {}
        self._guards = []
        for name, step in [("sQ", 16), ("sG", 16), ("sS", 16),
                           ("sW0", 16), ("sW1", 16),
                           ("sPE", 1), ("sDV", 1), ("sAC", 1)]:
            g = nc.semaphore(name)
            self.sems[name] = g.__enter__()
            self._guards.append(g)
            self.ctrs[name] = Ctr(step)

    def W(self, eng, name, val):
        if val and val > 0:
            sem = self.sems[name]
            self.lists[eng].append(lambda En, s=sem, v=val: En.wait_ge(s, v))

    def OP(self, eng, fn, sem=None):
        if sem is not None:
            c = self.ctrs[sem]
            c.v += c.step
            val = c.v
            s = self.sems[sem]

            def wrapped(En, f=fn, s=s, st=c.step):
                f(En).then_inc(s, st)
            self.lists[eng].append(wrapped)
            return val
        self.lists[eng].append(lambda En, f=fn: f(En))
        return None

    def close(self):
        for g in self._guards:
            g.__exit__(None, None, None)


class Alloc:
    def __init__(self, nc):
        self.nc = nc
        self.live = {}

    def sb(self, name, shape, dt, side="left"):
        g = self.nc.sbuf_tensor(name, shape, dt, side=side)
        t = g.__enter__()
        self.live[name] = g
        return t

    def ps(self, name, shape, dt):
        g = self.nc.psum_tensor(name, shape, dt)
        t = g.__enter__()
        self.live[name] = g
        return t

    def free(self, *names):
        order = list(self.live)
        for n in sorted(names, key=order.index, reverse=True):
            self.live.pop(n).__exit__(None, None, None)

    def close(self):
        for n in reversed(list(self.live)):
            self.live.pop(n).__exit__(None, None, None)


def build_nc():
    nc = bass.Bass("TRN2", target_bir_lowering=False)

    dx = nc.dram_tensor("x", [S, D], F32, kind="ExternalInput").ap()
    dwqkv = nc.dram_tensor("wqkv", [D, 3 * D], F32, kind="ExternalInput").ap()
    dwout = nc.dram_tensor("wout", [D, D], F32, kind="ExternalInput").ap()
    dwr = nc.dram_tensor("wr", [E, D], F32, kind="ExternalInput").ap()
    dw1 = nc.dram_tensor("w1", [E, D, FH], F32, kind="ExternalInput").ap()
    dw2 = nc.dram_tensor("w2", [E, FH, D], F32, kind="ExternalInput").ap()
    dbqkr = nc.dram_tensor("bqkr", [1, 2 * D], F32, kind="ExternalInput").ap()
    db1c = nc.dram_tensor("b1c", [128, 2 * E], F32, kind="ExternalInput").ap()
    dbv = nc.dram_tensor("bv", [1, D], F32, kind="ExternalInput").ap()
    dbo = nc.dram_tensor("bo", [1, D], F32, kind="ExternalInput").ap()
    dbr = nc.dram_tensor("br", [1, E], F32, kind="ExternalInput").ap()
    db2 = nc.dram_tensor("b2", [E, D], F32, kind="ExternalInput").ap()
    dident = nc.dram_tensor("identb", [128, 128], BF16, kind="ExternalInput").ap()
    dust = nc.dram_tensor("ustrb", [128, 128], BF16, kind="ExternalInput").ap()
    dmask = nc.dram_tensor("maskb", [128, 2048], BF16, kind="ExternalInput").ap()
    diota = nc.dram_tensor("iotaf", [128, E], F32, kind="ExternalInput").ap()
    dtrash = nc.dram_tensor("trashf", [128, TC], F32, kind="ExternalInput").ap()

    dperm = nc.dram_tensor("perm_scr", [PERM_ROWS, D], BF16, kind="Internal").ap()
    dyperm = nc.dram_tensor("yperm_scr", [PERM_ROWS, D], BF16, kind="Internal").ap()
    drbscr = nc.dram_tensor("rb_scr", [4 * HP, D], F32, kind="Internal").ap()
    dout = nc.dram_tensor("out", [S, D], F32, kind="ExternalOutput").ap()

    al = Alloc(nc)
    bd = Builder(nc)
    W, OP = bd.W, bd.OP

    cID = al.sb("cID", [128, 128], BF16)
    cUST = al.sb("cUST", [128, 128], BF16)
    cMASK = al.sb("cMASK", [128, 2048], BF16)
    cIOTA = al.sb("cIOTA", [128, E], F32)
    cTRASH = al.sb("cTRASH", [128, TC], F32)
    b1c = al.sb("sb_b1c", [128, 2 * E], F32)
    onesb = al.sb("onesb", [1, 128], BF16)
    onesw = al.sb("onesw", [1, 512], BF16)
    expsh = al.sb("expsh", [128, 1], F32)
    epst = al.sb("epst", [128, 1], F32)
    onescol = al.sb("onescol", [128, 1], BF16)
    onesf = al.sb("onesf", [1, 128], F32)
    zer32 = al.sb("zer32", [128, E], F32)
    bvb = al.sb("bvb", [1, D], BF16)
    bob = al.sb("bob", [1, D], BF16)
    brb = al.sb("brb", [1, E], BF16)
    wrtb = al.sb("wrtb", [128, 8 * E], BF16)
    destu = al.sb("destu", [128, TC], U32)
    prefix = al.sb("prefix", [1, 2 * E], F32)
    stats = al.sb("stats", [128, 16], F32)
    lgs = al.sb("lgs", [128, E], F32)
    ohraw = al.sb("ohraw", [128, E], F32)
    csum = al.sb("csum", [128, E], F32)
    t1s = al.sb("t1s", [128, E], F32)
    ohs = al.sb("ohs", [128, E], F32)
    ohb = al.sb("ohb", [128, E], BF16)
    rks = al.sb("rks", [128, E], F32)
    rkm = al.sb("rkm", [128, E], F32)
    cnts = al.sb("cnts", [1, E], F32)
    scal = al.sb("scal", [128, 8], F32)
    b2s = [al.sb(f"b2s{i}", [1, D], BF16) for i in range(2)]
    w1s, w2s = {}, {}
    for i in range(NEARLY):
        w1s[i] = al.sb(f"w1s{i}", [128, 8 * FH], BF16)
        w2s[i] = al.sb(f"w2s{i}", [128, 2 * D], BF16)
    xfm = al.sb("xfm", [128, TC * D], BF16)

    consts_v = 0
    for dst, src in [(cID, dident), (cUST, dust), (cMASK, dmask), (cIOTA, diota),
                     (cTRASH, dtrash), (b1c, db1c)]:
        consts_v = OP("sync", lambda En, d=dst, s=src: En.dma_start(out=d[:], in_=s), "sQ")

    bvb_v = OP("gpsimd", lambda En: En.dma_start(out=bvb[:], in_=dbv), "sG")
    OP("gpsimd", lambda En: En.dma_start(out=bob[:], in_=dbo), "sG")
    brb_v = OP("gpsimd", lambda En: En.dma_start(out=brb[:], in_=dbr), "sG")

    OP("vector", lambda En: En.memset(onesb[:], 1.0))
    OP("vector", lambda En: En.memset(onesw[:], 1.0))
    OP("vector", lambda En: En.memset(expsh[:], EXP_SHIFT))
    OP("vector", lambda En: En.memset(epst[:], LN_EPS))
    OP("vector", lambda En: En.memset(onescol[:], 1.0))
    OP("vector", lambda En: En.memset(onesf[:], 1.0))
    OP("vector", lambda En: En.memset(zer32[:], 0.0))
    ms_v = OP("vector", lambda En: En.memset(prefix[:], 0.0), "sDV")

    wrstg = al.sb("wrstg", [E, D], BF16, side="right")
    xstg = [al.sb(f"xstg{i}", [128, D], F32, side="right") for i in range(2)]
    xbstg = [al.sb(f"xbstg{i}", [128, D], BF16, side="right") for i in range(2)]

    wrstg_v = OP("gpsimd", lambda En: En.dma_start(out=wrstg[:], in_=dwr), "sG")

    pXt = [al.ps(f"pXt{i}", [128, 1024], BF16) for i in range(2)]
    xr = dx.rearrange("(c p) d -> c p d", p=128)
    xld, xcast, xtr, xev = [0] * TC, [0] * TC, [0] * TC, [0] * TC
    for i in range(TC):
        if i >= 2:
            W("sync", "sDV", xcast[i - 2])
        xld[i] = OP("sync", lambda En, i=i: En.dma_start(out=xstg[i % 2][:], in_=xr[i]), "sQ")
        W("vector", "sQ", xld[i])
        if i >= 2:
            W("vector", "sPE", xtr[i - 2])
        xcast[i] = OP("vector", lambda En, i=i: En.tensor_copy(xbstg[i % 2][:], xstg[i % 2][:]), "sDV")
        W("tensor", "sDV", xcast[i])
        if i >= 2:
            W("tensor", "sDV", xev[i - 2])
        for j in range(8):
            xtr[i] = OP("tensor", lambda En, i=i, j=j: En.transpose(
                pXt[i % 2][:, 128 * j:128 * (j + 1)],
                xbstg[i % 2][:, 128 * j:128 * (j + 1)], cID[:]), "sPE")
        W("vector", "sPE", xtr[i])
        xev[i] = OP("vector", lambda En, i=i: En.tensor_copy(
            bass.AP(xfm[:].tensor, xfm[:].offset + 128 * i,
                    [xfm[:].ap[0], [2048, 8], [1, 128]]),
            pXt[i % 2][:].rearrange("p (j c) -> p j c", c=128)), "sDV")
    xfm_v = xev[TC - 1]

    w1r = dw1.rearrange("e (k p) h -> e p k h", p=128)
    w2r = dw2.rearrange("e (k p) m -> e p k m", p=128)
    slot_gate = {}
    exp_loaded = {}

    b2_loaded = {}

    def emit_expert_load(e, extra_pe_gate=None, extra_dv_gate=None):
        s = e % NSLOT
        g = slot_gate.get(s)
        if g:
            W("gpsimd", "sPE", g)
        if extra_pe_gate:
            W("gpsimd", "sPE", extra_pe_gate)
        if extra_dv_gate:
            W("gpsimd", "sDV", extra_dv_gate)
        OP("gpsimd", lambda En, e=e, s=s: En.dma_start(out=w1s[s][:], in_=w1r[e]), "sG")
        exp_loaded[e] = OP("gpsimd", lambda En, e=e, s=s: En.dma_start(
            out=w2s[s][:], in_=w2r[e]), "sG")

    def emit_b2_load(e, pe_gate=None):
        if pe_gate:
            W("gpsimd", "sPE", pe_gate)
        b2_loaded[e] = OP("gpsimd", lambda En, e=e: En.dma_start(
            out=b2s[e % 2][:], in_=db2[e:e + 1, :]), "sG")

    pRtr = al.ps("pRtr", [128, 8 * E], BF16)
    W("tensor", "sG", wrstg_v)
    W("tensor", "sQ", consts_v)
    wrt_pe = 0
    for k in range(8):
        wrt_pe = OP("tensor", lambda En, k=k: En.transpose(
            pRtr[:, E * k:E * (k + 1)], wrstg[:, 128 * k:128 * (k + 1)], cID[0:E, 0:E]), "sPE")
    W("vector", "sPE", wrt_pe)
    wrtb_v = OP("vector", lambda En: En.tensor_copy(wrtb[:], pRtr[:]), "sDV")

    for e in range(NEARLY):
        emit_expert_load(e)
    for e in range(2):
        emit_b2_load(e)

    al.free("xbstg1", "xbstg0", "xstg1", "xstg0", "wrstg", "pRtr", "pXt0", "pXt1")

    qfm = al.sb("qfm", [128, TC * D], BF16, side="right")
    kfm = al.sb("kfm", [128, TC * D], BF16, side="right")
    vtm = al.sb("vtm", [128, TC * NH * 65], BF16, side="right")
    bqkrb = al.sb("bqkrb", [1, 2 * D], BF16, side="right")
    wvb = al.sb("wvb", [128, 8 * D], BF16, side="right")
    wqkb = al.sb("wqkb", [128, 8 * 2 * D], BF16, side="right")

    vones_v = OP("vector", lambda En: En.memset(vtm[:, 64:TC * NH * 65:65], 1.0), "sDV")

    wvr = dwqkv[:, 2 * D:3 * D].rearrange("(k p) m -> k p m", p=128)
    wv_v = 0
    for k in range(8):
        wv_v = OP("gpsimd", lambda En, k=k: En.dma_start(
            out=wvb[:, D * k:D * (k + 1)], in_=wvr[k]), "sG")
    bqkr_v = OP("gpsimd", lambda En: En.dma_start(out=bqkrb[:], in_=dbqkr), "sG")
    wqkr = dwqkv[:, 0:2 * D].rearrange("(k p) m -> k p m", p=128)
    wqk_v = [0] * 8
    for k in range(8):
        wqk_v[k] = OP("gpsimd", lambda En, k=k: En.dma_start(
            out=wqkb[:, 2048 * k:2048 * (k + 1)], in_=wqkr[k]), "sG")

    pQK = al.ps("pQK", [128, 2048], F32)
    pV = al.ps("pV", [128, 1024], F32)

    W("tensor", "sDV", xfm_v)
    W("tensor", "sG", wv_v)
    W("tensor", "sG", bvb_v)
    W("tensor", "sG", bqkr_v)
    W("tensor", "sDV", ms_v)
    W("tensor", "sDV", wrtb_v)

    slot_user = {}
    v_done_v = 0
    for tc in range(TC):
        for n2 in range(2):
            slot = (tc * 2 + n2) % 2
            g = slot_user.get(("v", slot))
            if g:
                W("tensor", "sDV", g)
            for k in range(8):
                OP("tensor", lambda En, k=k, tc=tc, n2=n2, slot=slot: En.matmul(
                    pV[:, 512 * slot:512 * (slot + 1)],
                    xfm[:, 2048 * k + 128 * tc:2048 * k + 128 * (tc + 1)],
                    wvb[:, D * k + 512 * n2:D * k + 512 * (n2 + 1)],
                    start=(k == 0), stop=False), "sPE")
            pe_v = OP("tensor", lambda En, n2=n2, slot=slot: En.matmul(
                pV[:, 512 * slot:512 * (slot + 1)],
                onesb[:], bvb[0:1, 512 * n2:512 * (n2 + 1)],
                start=False, stop=True), "sPE")
            W("vector", "sPE", pe_v)
            dv_v = OP("vector", lambda En, tc=tc, n2=n2, slot=slot: En.tensor_copy(
                vtm[:, 1040 * tc + 520 * n2:1040 * tc + 520 * (n2 + 1)]
                    .rearrange("p (h c) -> p h c", c=65)[:, :, 0:64],
                pV[:, 512 * slot:512 * (slot + 1)].rearrange("p (h c) -> p h c", c=64)), "sDV")
            slot_user[("v", slot)] = dv_v
            v_done_v = dv_v

    qk_done_v = 0
    for hp in range(HP):
        for which in (0, 1):
            for n in range(4):
                slot = (hp * 4 + n) % 2 + 2 * which
                g = slot_user.get(("qk", slot))
                if g:
                    W("tensor", "sAC", g)
                pe_v = 0
                for k in range(8):
                    W("tensor", "sG", wqk_v[k])
                    ofc = 2048 * k + 1024 * which + 128 * hp
                    pe_v = OP("tensor", lambda En, k=k, ofc=ofc, n=n, slot=slot: En.matmul(
                        pQK[:, 512 * slot:512 * (slot + 1)],
                        wqkb[:, ofc:ofc + 128],
                        xfm[:, 2048 * k + 512 * n:2048 * k + 512 * (n + 1)],
                        start=(k == 0), stop=False), "sPE")
                boff = 1024 * which + 128 * hp
                pe_v = OP("tensor", lambda En, boff=boff, slot=slot: En.matmul(
                    pQK[:, 512 * slot:512 * (slot + 1)],
                    bqkrb[0:1, boff:boff + 128], onesw[:],
                    start=False, stop=True), "sPE")
                dst = qfm if which == 0 else kfm
                W("scalar", "sPE", pe_v)
                ac_v = OP("scalar", lambda En, dst=dst, hp=hp, n=n, slot=slot: En.copy(
                    dst[:, 2048 * hp + 512 * n:2048 * hp + 512 * (n + 1)],
                    pQK[:, 512 * slot:512 * (slot + 1)]), "sAC")
                slot_user[("qk", slot)] = ac_v
                qk_done_v = ac_v

    al.free("wqkb", "wvb", "bqkrb", "pQK", "pV")

    rr = al.sb("rr", [128, 1024], F32, side="right")
    rb = al.sb("rb", [128, 2048], F32, side="right")
    ring = al.sb("ring", [128, 4096], BF16, side="right")
    psc = al.ps("psc", [128, 2048], F32)
    poAs = [al.ps(f"poA{i}", [65, 512], F32) for i in range(2)]
    poBs = [al.ps(f"poB{i}", [65, 512], F32) for i in range(2)]
    attnout = xfm

    W("tensor", "sAC", qk_done_v)
    W("tensor", "sDV", v_done_v)
    W("tensor", "sDV", vones_v)
    W("scalar", "sDV", ms_v)

    slot_exp = {}
    ring_pv = {}
    o_mult_done = {0: 0, 1: 0}
    rr_store_last = 0
    attn_done_v = 0
    itn = 0

    for qc in range(4):
        kv0 = 4 * qc
        for hp in range(HP):
            par = itn % 2
            poA, poB = poAs[par], poBs[par]
            sc_v, exp_v, pv_v = {}, {}, {}

            def emit_scores(kvc, qc=qc, hp=hp, sc_v=None):
                slot = kvc % 2
                if slot_exp.get(slot):
                    W("tensor", "sAC", slot_exp[slot])
                diag = kvc < 4 * qc + 4
                o = kvc - 4 * qc
                last = None
                for sub in (0, 1):
                    base = 64 * sub
                    col = 1024 * slot + 512 * sub
                    last = OP("tensor", lambda En, kvc=kvc, col=col, base=base, diag=diag, hp=hp, qc=qc: En.matmul(
                        psc[:, col:col + 512],
                        kfm[base:base + 64, 2048 * hp + 128 * kvc:2048 * hp + 128 * (kvc + 1)],
                        qfm[base:base + 64, 2048 * hp + 512 * qc:2048 * hp + 512 * (qc + 1)],
                        start=True, stop=(not diag), tile_position=(base, 0)), "sPE")
                if diag:
                    for sub in (0, 1):
                        col = 1024 * slot + 512 * sub
                        last = OP("tensor", lambda En, col=col, o=o: En.matmul(
                            psc[:, col:col + 512], cID[:],
                            cMASK[:, 512 * o:512 * (o + 1)],
                            start=False, stop=True), "sPE")
                sc_v[kvc] = last

            def emit_exp(kvc, sc_v=None, exp_v=None):
                slot = kvc % 2
                rslot = kvc % 4
                W("scalar", "sPE", sc_v[kvc])
                if ring_pv.get(rslot):
                    W("scalar", "sPE", ring_pv[rslot])
                exp_v[kvc] = OP("scalar", lambda En, slot=slot, rslot=rslot: En.activation(
                    ring[:, 1024 * rslot:1024 * (rslot + 1)],
                    psc[:, 1024 * slot:1024 * (slot + 1)],
                    AF.Exp, bias=expsh[:], scale=ATTN_SCALE), "sAC")
                slot_exp[slot] = exp_v[kvc]

            def emit_pv(kvc, qc=qc, hp=hp, kv0=kv0, par=par, poA=poA, poB=poB, exp_v=None, pv_v=None):
                rslot = kvc % 4
                W("tensor", "sAC", exp_v[kvc])
                if kvc == kv0 and o_mult_done[par]:
                    W("tensor", "sDV", o_mult_done[par])
                OP("tensor", lambda En, kvc=kvc, rslot=rslot, hp=hp, kv0=kv0: En.matmul(
                    poA[:], vtm[:, 1040 * kvc + 65 * (2 * hp):1040 * kvc + 65 * (2 * hp) + 65],
                    ring[:, 1024 * rslot:1024 * rslot + 512],
                    start=(kvc == kv0), stop=(kvc == TC - 1)), "sPE")
                pv_v[kvc] = OP("tensor", lambda En, kvc=kvc, rslot=rslot, hp=hp, kv0=kv0: En.matmul(
                    poB[:], vtm[:, 1040 * kvc + 65 * (2 * hp + 1):1040 * kvc + 65 * (2 * hp + 1) + 65],
                    ring[:, 1024 * rslot + 512:1024 * (rslot + 1)],
                    start=(kvc == kv0), stop=(kvc == TC - 1)), "sPE")
                ring_pv[rslot] = pv_v[kvc]

            for kvc in range(kv0, TC):
                emit_scores(kvc, sc_v=sc_v)
                emit_exp(kvc, sc_v=sc_v, exp_v=exp_v)
                if kvc > kv0:
                    emit_pv(kvc - 1, exp_v=exp_v, pv_v=pv_v)
            emit_pv(TC - 1, exp_v=exp_v, pv_v=pv_v)
            pv_all = pv_v[TC - 1]

            W("vector", "sPE", pv_all)
            if rr_store_last:
                W("vector", "sQ", rr_store_last)
            OP("vector", lambda En, poA=poA: En.reciprocal(rr[64:65, 0:512], poA[64:65, :]))
            rc1 = OP("vector", lambda En, poB=poB: En.reciprocal(rr[64:65, 512:1024], poB[64:65, :]), "sDV")

            W("sync", "sDV", rc1)
            st_rr = OP("sync", lambda En, itn=itn: En.dma_start(
                out=drbscr[itn:itn + 1, :], in_=rr[64:65, 0:1024]), "sQ")
            rr_store_last = st_rr
            W("sync", "sQ", st_rr)
            if o_mult_done[par]:
                W("sync", "sDV", o_mult_done[par])
            base_ap = drbscr[itn:itn + 1, :]
            bcast_v = OP("sync", lambda En, itn=itn, par=par, base_ap=base_ap: En.dma_start(
                out=rb[0:64, 1024 * par:1024 * (par + 1)],
                in_=bass.AP(base_ap.tensor, base_ap.offset, [[0, 64], [1, 1024]])), "sQ")

            W("vector", "sQ", bcast_v)
            OP("vector", lambda En, hp=hp, qc=qc, par=par, poA=poA: En.tensor_tensor(
                attnout[0:64, 2048 * hp + 512 * qc:2048 * hp + 512 * (qc + 1)],
                poA[0:64, :], rb[0:64, 1024 * par:1024 * par + 512], ALU.mult))
            o_mult_done[par] = OP("vector", lambda En, hp=hp, qc=qc, par=par, poB=poB: En.tensor_tensor(
                attnout[64:128, 2048 * hp + 512 * qc:2048 * hp + 512 * (qc + 1)],
                poB[0:64, :], rb[0:64, 1024 * par + 512:1024 * (par + 1)], ALU.mult), "sDV")
            attn_done_v = o_mult_done[par]
            itn += 1

    attn_pe_v = bd.ctrs["sPE"].v
    al.free("psc", "poA0", "poA1", "poB0", "poB1", "ring", "rb", "rr", "vtm", "kfm", "qfm")

    woutb = al.sb("woutb", [128, 8 * D], BF16, side="right")
    x1fm = [al.sb(f"x1fm{i}", [128, D], BF16, side="right") for i in range(2)]
    x1bf = al.sb("x1bf", [128, TC * D], BF16, side="right")
    sqscr = al.sb("sqscr", [128, D], F32, side="right")
    x1pre = [al.sb(f"x1pre{i}", [128, D], F32, side="right") for i in range(2)]
    x2pre = al.sb("x2pre", [128, D], F32, side="right")
    xres = [al.sb(f"xres{i}", [128, D], F32, side="right") for i in range(2)]
    outstg = [al.sb(f"outstg{i}", [128, D], F32, side="right") for i in range(2)]
    ygat = [al.sb(f"ygat{i}", [128, D], BF16, side="right") for i in range(2)]
    xet = [al.sb(f"xet{i}", [128, D], BF16, side="right") for i in range(3)]
    hbf = [al.sb(f"hbf{i}", [128, FH], BF16, side="right") for i in range(2)]
    for i in range(NEARLY, 7):
        w1s[i] = al.sb(f"w1s{i}", [128, 8 * FH], BF16, side="right")
        w2s[i] = al.sb(f"w2s{i}", [128, 2 * D], BF16, side="right")

    pOut = [al.ps(f"pOut{i}", [128, 1024], F32) for i in range(2)]
    pTr = al.ps("pTr", [128, 1024], BF16)
    pRt = [al.ps(f"pRt{i}", [128, 512], F32) for i in range(2)]

    wor = dwout.rearrange("(k p) m -> k p m", p=128)
    wout_k = [0] * 8
    W("gpsimd", "sPE", attn_pe_v)
    for k in range(8):
        wout_k[k] = OP("gpsimd", lambda En, k=k: En.dma_start(
            out=woutb[:, D * k:D * (k + 1)], in_=wor[k]), "sG")

    for e in range(NEARLY, 7):
        emit_expert_load(e, extra_pe_gate=attn_pe_v, extra_dv_gate=attn_done_v)

    def emit_ln(src, src_eng, src_v, dst_fn):
        W("vector", src_eng, src_v)
        s1 = OP("vector", lambda En: En.reduce_sum(
            stats[:, 0:1], src[:], axis=AX.X, negate=True), "sDV")
        W("scalar", src_eng, src_v)
        s2 = OP("scalar", lambda En: En.activation(
            sqscr[:], src[:], AF.Square, accum_out=stats[:, 1:2]), "sAC")
        W("vector", "sDV", s1)
        W("vector", "sAC", s2)
        OP("vector", lambda En: En.tensor_scalar(
            stats[:, 2:3], stats[:, 0:1], 1.0 / D, None, ALU.mult))
        s3 = OP("vector", lambda En: En.tensor_scalar(
            stats[:, 3:4], stats[:, 1:2], 1.0 / D, None, ALU.mult), "sDV")
        W("vector", "sDV", s3)
        s4 = OP("vector", lambda En: En.tensor_tensor(
            stats[:, 4:5], stats[:, 2:3], stats[:, 2:3], ALU.mult), "sDV")
        W("vector", "sDV", s4)
        s5 = OP("vector", lambda En: En.tensor_tensor(
            stats[:, 5:6], stats[:, 3:4], stats[:, 4:5], ALU.subtract), "sDV")
        W("scalar", "sDV", s5)
        s6 = OP("scalar", lambda En: En.activation(
            stats[:, 6:7], stats[:, 5:6], AF.Sqrt, bias=epst[:]), "sAC")
        W("vector", "sAC", s6)
        s7 = OP("vector", lambda En: En.reciprocal(stats[:, 7:8], stats[:, 6:7]), "sDV")
        W("vector", "sDV", s7)
        return dst_fn()

    if DEBUG_DUMPS:
        dbgpre = nc.dram_tensor("dbg_x1pre", [S, D], F32, kind="ExternalOutput").ap()
        dbgprer = dbgpre.rearrange("(c p) d -> c p d", p=128)
    outp_free = {0: attn_done_v, 1: attn_done_v}
    pTr_free = 0
    pRt_free = {0: attn_done_v, 1: attn_done_v}
    x1fm_free = {0: 0, 1: 0}
    x1pre_free = {0: 0, 1: 0}
    xres_free = {0: 0, 1: 0}
    x1bf_v = [0] * TC
    dest_done = [0] * TC
    scat_done = [0] * TC
    prev_dest_chain = 0
    prev_cnt_pe = 0
    lg_pe = 0

    def p3_stage1(tc):
        if tc < 2:
            W("sync", "sDV", attn_done_v)
        if xres_free[tc % 2]:
            W("sync", "sDV", xres_free[tc % 2])
        xrld = OP("sync", lambda En, tc=tc: En.dma_start(out=xres[tc % 2][:], in_=xr[tc]), "sQ")

        resadd_v = 0
        for n2 in range(2):
            W("tensor", "sDV", outp_free[tc % 2])
            for k in range(8):
                if tc == 0:
                    W("tensor", "sG", wout_k[k])
                OP("tensor", lambda En, k=k, tc=tc, n2=n2: En.matmul(
                    pOut[tc % 2][:, 512 * n2:512 * (n2 + 1)],
                    attnout[:, 2048 * k + 128 * tc:2048 * k + 128 * (tc + 1)],
                    woutb[:, 1024 * k + 512 * n2:1024 * k + 512 * (n2 + 1)],
                    start=(k == 0), stop=False), "sPE")
            pe_v = OP("tensor", lambda En, tc=tc, n2=n2: En.matmul(
                pOut[tc % 2][:, 512 * n2:512 * (n2 + 1)],
                onesb[:], bob[0:1, 512 * n2:512 * (n2 + 1)],
                start=False, stop=True), "sPE")
            W("vector", "sPE", pe_v)
            W("vector", "sQ", xrld)
            if n2 == 0 and x1pre_free[tc % 2]:
                W("vector", "sDV", x1pre_free[tc % 2])
            resadd_v = OP("vector", lambda En, tc=tc, n2=n2: En.tensor_tensor(
                x1pre[tc % 2][:, 512 * n2:512 * (n2 + 1)],
                pOut[tc % 2][:, 512 * n2:512 * (n2 + 1)],
                xres[tc % 2][:, 512 * n2:512 * (n2 + 1)], ALU.add), "sDV")
        outp_free[tc % 2] = resadd_v
        xres_free[tc % 2] = resadd_v

        if DEBUG_DUMPS:
            W("sync", "sDV", resadd_v)
            dbg_v = OP("sync", lambda En, tc=tc: En.dma_start(
                out=dbgprer[tc], in_=x1pre[tc % 2][:]), "sQ")
            W("vector", "sQ", dbg_v)

        def ln1_dst(tc=tc):
            return OP("vector", lambda En: En.tensor_scalar(
                x1bf[:, D * tc:D * (tc + 1)], x1pre[tc % 2][:],
                stats[:, 2:3], stats[:, 7:8], ALU.add, ALU.mult), "sDV")
        x1bf_v[tc] = emit_ln(x1pre[tc % 2], "sDV", resadd_v, ln1_dst)
        x1pre_free[tc % 2] = x1bf_v[tc]

    def p3_stage2(tc):
        nonlocal pTr_free, prev_dest_chain, prev_cnt_pe, lg_pe
        W("tensor", "sDV", x1bf_v[tc])
        if pTr_free:
            W("tensor", "sDV", pTr_free)
        tr_v = 0
        for j in range(8):
            tr_v = OP("tensor", lambda En, tc=tc, j=j: En.transpose(
                pTr[:, 128 * j:128 * (j + 1)],
                x1bf[:, D * tc + 128 * j:D * tc + 128 * (j + 1)], cID[:]), "sPE")
        W("vector", "sPE", tr_v)
        if x1fm_free[tc % 2]:
            W("vector", "sPE", x1fm_free[tc % 2])
        x1fm_v = OP("vector", lambda En, tc=tc: En.tensor_copy(x1fm[tc % 2][:], pTr[:]), "sDV")
        pTr_free = x1fm_v

        W("tensor", "sDV", x1fm_v)
        W("tensor", "sDV", pRt_free[tc % 2])
        W("tensor", "sG", brb_v)
        for k in range(8):
            OP("tensor", lambda En, k=k, tc=tc: En.matmul(
                pRt[tc % 2][:, 0:E],
                x1fm[tc % 2][:, 128 * k:128 * (k + 1)],
                wrtb[:, E * k:E * (k + 1)],
                start=(k == 0), stop=False), "sPE")
        lg_pe = OP("tensor", lambda En, tc=tc: En.matmul(
            pRt[tc % 2][:, 0:E], onesb[:], brb[:], start=False, stop=True), "sPE")
        x1fm_free[tc % 2] = lg_pe

        W("vector", "sPE", lg_pe)
        if prev_dest_chain:
            W("vector", "sDV", prev_dest_chain)
        c1 = OP("vector", lambda En, tc=tc: En.tensor_copy(lgs[:], pRt[tc % 2][:, 0:E]), "sDV")
        W("vector", "sDV", c1)
        c2 = OP("vector", lambda En: En.reduce_max(stats[:, 8:9], lgs[:], axis=AX.X), "sDV")
        W("vector", "sDV", c2)
        c3 = OP("vector", lambda En: En.tensor_scalar(
            ohraw[:], lgs[:], stats[:, 8:9], None, ALU.is_equal), "sDV")
        W("vector", "sDV", c3)
        c4 = OP("vector", lambda En: En.tensor_tensor_scan(
            csum[:], ohraw[:], zer32[:], 0.0, ALU.add, ALU.add), "sDV")
        W("vector", "sDV", c4)
        c6 = OP("vector", lambda En: En.scalar_tensor_tensor(
            ohs[:], csum[:], 1.0, ohraw[:], ALU.is_equal, ALU.mult), "sDV")
        W("vector", "sDV", c6)
        if prev_cnt_pe:
            W("vector", "sPE", prev_cnt_pe)
        ohb_v = OP("vector", lambda En: En.tensor_copy(ohb[:], ohs[:]), "sDV")

        W("tensor", "sDV", ohb_v)
        OP("tensor", lambda En, tc=tc: En.matmul(
            pRt[tc % 2][:, 64:64 + E], cUST[:], ohb[:], start=True, stop=False), "sPE")
        OP("tensor", lambda En, tc=tc: En.matmul(
            pRt[tc % 2][:, 64:64 + E], onesf[:],
            prefix[0:1, E * (tc % 2):E * (tc % 2) + E],
            start=False, stop=True), "sPE")
        cnt_pe = OP("tensor", lambda En, tc=tc: En.matmul(
            pRt[tc % 2][0:1, 128:128 + E], onescol[:], ohb[:], start=True, stop=True), "sPE")
        prev_cnt_pe = cnt_pe

        W("vector", "sPE", cnt_pe)
        d1 = OP("vector", lambda En, tc=tc: En.tensor_copy(rks[:], pRt[tc % 2][:, 64:64 + E]), "sDV")
        d2 = OP("vector", lambda En, tc=tc: En.tensor_copy(cnts[:], pRt[tc % 2][0:1, 128:128 + E]), "sDV")
        pRt_free[tc % 2] = d2
        W("vector", "sDV", d2)
        OP("vector", lambda En, tc=tc: En.tensor_tensor(
            prefix[0:1, E * ((tc + 1) % 2):E * ((tc + 1) % 2) + E],
            prefix[0:1, E * (tc % 2):E * (tc % 2) + E], cnts[:], ALU.add))
        OP("vector", lambda En: En.tensor_tensor(rkm[:], rks[:], ohs[:], ALU.mult))
        d3 = OP("vector", lambda En: En.tensor_tensor(t1s[:], cIOTA[:], ohs[:], ALU.mult), "sDV")
        W("vector", "sDV", d3)
        OP("vector", lambda En: En.reduce_sum(scal[:, 0:1], rkm[:], axis=AX.X))
        d4 = OP("vector", lambda En: En.reduce_sum(scal[:, 1:2], t1s[:], axis=AX.X), "sDV")
        W("vector", "sDV", d4)
        OP("vector", lambda En: En.tensor_scalar(
            scal[:, 2:3], scal[:, 1:2], 128.0, scal[:, 0:1], ALU.mult, ALU.add))
        d5 = OP("vector", lambda En: En.tensor_scalar(
            scal[:, 3:4], scal[:, 0:1], 127.5, None, ALU.is_gt), "sDV")
        W("vector", "sDV", d5)
        d6 = OP("vector", lambda En, tc=tc: En.tensor_tensor(
            scal[:, 4:5], cTRASH[:, tc:tc + 1], scal[:, 2:3], ALU.subtract), "sDV")
        W("vector", "sDV", d6)
        d8 = OP("vector", lambda En: En.scalar_tensor_tensor(
            scal[:, 6:7], scal[:, 4:5], scal[:, 3:4], scal[:, 2:3],
            ALU.mult, ALU.add), "sDV")
        W("vector", "sDV", d8)
        dest_done[tc] = OP("vector", lambda En, tc=tc: En.tensor_copy(
            destu[:, tc:tc + 1], scal[:, 6:7]), "sDV")
        prev_dest_chain = dest_done[tc]

        W("gpsimd", "sDV", dest_done[tc])
        scat_done[tc] = OP("gpsimd", lambda En, tc=tc: En.indirect_dma_start(
            out=dperm, out_offset=IOX(destu[:, tc:tc + 1], 0),
            in_=x1bf[:, D * tc:D * (tc + 1)], in_offset=None), "sG")

    for tc in range(TC + 2):
        if tc < TC:
            p3_stage1(tc)
        if tc >= 2:
            p3_stage2(tc - 2)

    p3_dve_last = dest_done[TC - 1]
    p3_pe_last = bd.ctrs["sPE"].v
    al.free("pOut0", "pOut1", "pTr", "pRt0", "pRt1")
    al.free("xfm")
    for i in range(7, NSLOT):
        w1s[i] = al.sb(f"w1s{i}", [128, 8 * FH], BF16)
        w2s[i] = al.sb(f"w2s{i}", [128, 2 * D], BF16)
    wstg = [al.sb(f"wstg{i}", [128, 2048], F32) for i in range(3)]
    slot_gate[7] = p3_pe_last

    w2cast = {}
    w2stage = {}
    wstg_q = {}

    def emit_w1_only(e, extra_pe_gate=None, extra_dv_gate=None):
        s = e % NSLOT
        g = slot_gate.get(s)
        if g:
            W("gpsimd", "sPE", g)
        if extra_pe_gate:
            W("gpsimd", "sPE", extra_pe_gate)
        if extra_dv_gate:
            W("gpsimd", "sDV", extra_dv_gate)
        exp_loaded[e] = OP("gpsimd", lambda En, e=e, s=s: En.dma_start(
            out=w1s[s][:], in_=w1r[e]), "sG")

    def emit_w2_stage(e, pe_gate=None):
        if pe_gate:
            W("scalar", "sPE", pe_gate)
        if e >= 10:
            W("scalar", "sAC", w2cast[e - 3])
        w2stage[e] = OP("scalar", lambda En, e=e: En.dma_start(
            out=wstg[e % 3][:], in_=w2r[e]), "sW1")

    def emit_w2_cast(e):
        s = e % NSLOT
        W("scalar", "sW1", w2stage[e])
        g = slot_gate.get(s)
        if g:
            W("scalar", "sPE", g)
        w2cast[e] = OP("scalar", lambda En, e=e, s=s: En.copy(
            w2s[s][:], wstg[e % 3][:]), "sAC")

    emit_w1_only(7, extra_pe_gate=p3_pe_last)
    emit_w2_stage(7, pe_gate=p3_pe_last)
    emit_w2_stage(8, pe_gate=p3_pe_last)

    pXe = [al.ps(f"pXe{i}", [128, 1024], BF16) for i in range(2)]
    pH = [al.ps(f"pH{i}", [128, 128], F32) for i in range(2)]
    pY = [al.ps(f"pY{i}", [128, 1024], F32) for i in range(2)]
    xef = x1fm
    ybf = ygat

    permr = dperm.rearrange("(e p) d -> e p d", p=128)
    ypermr = dyperm.rearrange("(e p) d -> e p d", p=128)

    xet_free = {0: 0, 1: 0, 2: 0}
    pXe_free = {0: 0, 1: 0}
    xef_free = {0: lg_pe, 1: lg_pe}
    xef_v = {0: 0, 1: 0}
    pH_free = {0: 0, 1: 0}
    pY_free = {0: 0, 1: 0}
    hbf_free = {0: 0, 1: 0}
    ybf_free = {0: 0, 1: 0}
    ystore = [0] * E

    W("tensor", "sDV", p3_dve_last)
    W("sync", "sG", scat_done[TC - 1])

    xe_ld = {}

    def emit_xe_load(e):
        sx = e % 3
        if xet_free[sx]:
            W("sync", "sPE", xet_free[sx])
        xe_ld[e] = OP("sync", lambda En, e=e, sx=sx: En.dma_start(
            out=xet[sx][:], in_=permr[e]), "sQ")

    emit_xe_load(0)

    def p4_front(e):
        sl = e % 2
        sx = e % 3
        if e + 1 < E:
            emit_xe_load(e + 1)
        W("tensor", "sQ", xe_ld[e])
        if pXe_free[sl]:
            W("tensor", "sDV", pXe_free[sl])
        tr_v = 0
        for j in range(8):
            tr_v = OP("tensor", lambda En, sl=sl, sx=sx, j=j: En.transpose(
                pXe[sl][:, 128 * j:128 * (j + 1)], xet[sx][:, 128 * j:128 * (j + 1)], cID[:]), "sPE")
        xet_free[sx] = tr_v
        W("vector", "sPE", tr_v)
        if xef_free[sl]:
            W("vector", "sPE", xef_free[sl])
        xef_v[sl] = OP("vector", lambda En, sl=sl: En.tensor_copy(xef[sl][:], pXe[sl][:]), "sDV")
        pXe_free[sl] = xef_v[sl]

    def p4_back(e):
        sl = e % 2
        W("tensor", "sG", exp_loaded[e])
        W("tensor", "sDV", xef_v[sl])
        h_ac = {}
        h_pe_last = 0
        for hc in range(2):
            if pH_free[hc]:
                W("tensor", "sAC", pH_free[hc])
            h_pe = 0
            for k in range(8):
                h_pe = OP("tensor", lambda En, sl=sl, e=e, hc=hc, k=k: En.matmul(
                    pH[hc][:],
                    w1s[e % NSLOT][:, FH * k + 128 * hc:FH * k + 128 * (hc + 1)],
                    xef[sl][:, 128 * k:128 * (k + 1)],
                    start=(k == 0), stop=(k == 7)), "sPE")
            h_pe_last = h_pe
            W("scalar", "sPE", h_pe)
            if hbf_free[sl]:
                W("scalar", "sPE", hbf_free[sl])
            hb_v = OP("scalar", lambda En, sl=sl, e=e, hc=hc: En.activation(
                hbf[sl][:, 128 * hc:128 * (hc + 1)], pH[hc][:],
                AF.Relu, bias=b1c[:, 2 * e + hc:2 * e + hc + 1]), "sAC")
            pH_free[hc] = hb_v
            h_ac[hc] = hb_v
        xef_free[sl] = h_pe_last

        if 7 <= e + 7 < E:
            emit_w2_cast(e + 7)

        W("tensor", "sAC", h_ac[1])
        W("tensor", "sAC", h_ac[0])
        W("tensor", "sG", b2_loaded[e])
        if e >= 7:
            W("tensor", "sAC", w2cast[e])
        y_pe = 0
        for n2 in range(2):
            if pY_free[sl] and n2 == 0:
                W("tensor", "sAC", pY_free[sl])
            for hc in range(2):
                OP("tensor", lambda En, sl=sl, e=e, n2=n2, hc=hc: En.matmul(
                    pY[sl][:, 512 * n2:512 * (n2 + 1)],
                    hbf[sl][:, 128 * hc:128 * (hc + 1)],
                    w2s[e % NSLOT][:, D * hc + 512 * n2:D * hc + 512 * (n2 + 1)],
                    start=(hc == 0), stop=False), "sPE")
            y_pe = OP("tensor", lambda En, sl=sl, e=e, n2=n2: En.matmul(
                pY[sl][:, 512 * n2:512 * (n2 + 1)],
                onesb[:], b2s[e % 2][0:1, 512 * n2:512 * (n2 + 1)],
                start=False, stop=True), "sPE")
        hbf_free[sl] = y_pe
        slot_gate[e % NSLOT] = y_pe

        W("scalar", "sPE", y_pe)
        if ybf_free[sl]:
            W("scalar", "sW0", ybf_free[sl])
        yb_v = OP("scalar", lambda En, sl=sl: En.activation(
            ybf[sl][:], pY[sl][:], AF.Relu), "sAC")
        pY_free[sl] = yb_v
        W("sync", "sAC", yb_v)
        ystore[e] = OP("sync", lambda En, sl=sl, e=e: En.dma_start(
            out=ypermr[e], in_=ybf[sl][:]), "sW0")
        ybf_free[sl] = ystore[e]

        if e + 8 < E:
            emit_w1_only(e + 8)
        if e + 9 < E:
            emit_w2_stage(e + 9)
        if e + 2 < E:
            emit_b2_load(e + 2, pe_gate=y_pe)

    for e in range(E + 1):
        if e < E:
            p4_front(e)
        if e >= 1:
            p4_back(e - 1)

    out_free = {0: 0, 1: 0}
    ygat_gate = [0] * TC
    x2_gate = 0
    st_v = 0

    W("gpsimd", "sW0", ystore[E - 1])
    outr = dout.rearrange("(c p) d -> c p d", p=128)
    for tc in range(TC):
        sl = tc % 2
        if tc >= 2:
            W("gpsimd", "sDV", ygat_gate[tc - 2])
        g_v = OP("gpsimd", lambda En, tc=tc, sl=sl: En.indirect_dma_start(
            out=ygat[sl][:], out_offset=None,
            in_=dyperm, in_offset=IOX(destu[:, tc:tc + 1], 0)), "sG")
        W("vector", "sG", g_v)
        if x2_gate:
            W("vector", "sDV", x2_gate)
        add_v = OP("vector", lambda En, tc=tc, sl=sl: En.tensor_tensor(
            x2pre[:], x1bf[:, D * tc:D * (tc + 1)], ygat[sl][:], ALU.add), "sDV")
        ygat_gate[tc] = add_v

        def ln2_dst(tc=tc, sl=sl):
            if out_free[sl]:
                W("vector", "sQ", out_free[sl])
            return OP("vector", lambda En: En.tensor_scalar(
                outstg[sl][:], x2pre[:],
                stats[:, 2:3], stats[:, 7:8], ALU.add, ALU.mult), "sDV")
        ln_v = emit_ln(x2pre, "sDV", add_v, ln2_dst)
        x2_gate = ln_v
        W("sync", "sDV", ln_v)
        st_v = OP("sync", lambda En, tc=tc, sl=sl: En.dma_start(
            out=outr[tc], in_=outstg[sl][:]), "sQ")
        out_free[sl] = st_v

    if DEBUG_DUMPS:
        dba = nc.dram_tensor("dbg_attn", [128, TC * D], BF16, kind="ExternalOutput").ap()
        dbx = nc.dram_tensor("dbg_x1", [128, TC * D], BF16, kind="ExternalOutput").ap()
        dbd = nc.dram_tensor("dbg_destu", [128, TC], U32, kind="ExternalOutput").ap()
        W("sync", "sQ", st_v)
        st_v = OP("sync", lambda En: En.dma_start(out=dba, in_=xfm[:]), "sQ")
        st_v = OP("sync", lambda En: En.dma_start(out=dbx, in_=x1bf[:]), "sQ")
        st_v = OP("sync", lambda En: En.dma_start(out=dbd, in_=destu[:]), "sQ")

    W("sync", "sQ", st_v)

    with nc.Block() as block:
        for eng in ("sync", "scalar", "vector", "tensor", "gpsimd"):
            fns = bd.lists[eng]
            getattr(block, eng)(lambda En, fns=fns: [f(En) for f in fns])

    bd.close()
    al.close()
    return nc


_NC_CACHE = None


def _get_nc():
    global _NC_CACHE
    if _NC_CACHE is None:
        _NC_CACHE = build_nc()
    return _NC_CACHE


def kernel(L0_attn_x, L0_attn_w_qkv, L0_attn_b_qkv, L0_attn_w_out, L0_attn_b_out,
           L0_ffn_router_w, L0_ffn_router_b, L0_ffn_weights1, L0_ffn_bias1,
           L0_ffn_weights2, L0_ffn_bias2):
    x = np.asarray(L0_attn_x, dtype=np.float32)
    wqkv = np.ascontiguousarray(np.asarray(L0_attn_w_qkv, dtype=np.float32))
    bqkv = np.asarray(L0_attn_b_qkv, dtype=np.float32)
    wout = np.ascontiguousarray(np.asarray(L0_attn_w_out, dtype=np.float32))
    bout = np.asarray(L0_attn_b_out, dtype=np.float32)
    wr = np.ascontiguousarray(np.asarray(L0_ffn_router_w, dtype=np.float32))
    br = np.asarray(L0_ffn_router_b, dtype=np.float32)
    w1 = np.ascontiguousarray(np.asarray(L0_ffn_weights1, dtype=np.float32))
    b1 = np.asarray(L0_ffn_bias1, dtype=np.float32)
    w2 = np.ascontiguousarray(np.asarray(L0_ffn_weights2, dtype=np.float32))
    b2 = np.ascontiguousarray(np.asarray(L0_ffn_bias2, dtype=np.float32))

    bqkr = np.ascontiguousarray(bqkv[0:2048][None, :])
    b1c = np.ascontiguousarray(
        np.stack(sum([[b1[e, 0:128], b1[e, 128:256]] for e in range(E)], []), axis=1))
    bv = np.ascontiguousarray(bqkv[2048:3072][None, :])
    bo = np.ascontiguousarray(bout[None, :])
    brr = np.ascontiguousarray(br[None, :])

    identb = np.eye(128, dtype=ml_dtypes.bfloat16)
    ustrb = np.triu(np.ones((128, 128), np.float32), k=1).astype(ml_dtypes.bfloat16)
    ii = np.arange(128)[:, None]
    jj = np.arange(512)[None, :]
    maskb = np.concatenate(
        [np.where(ii + 128 * o < jj, np.float32(NEG_BIG), np.float32(0.0))
         for o in range(4)], axis=1).astype(ml_dtypes.bfloat16)
    iotaf = np.tile(np.arange(E, dtype=np.float32), (128, 1))
    trashf = (E * CAP + 128 * np.arange(TC)[None, :]
              + np.arange(128)[:, None]).astype(np.float32)

    nc = _get_nc()
    common = dict(wqkv=wqkv, wout=wout, wr=wr, w1=w1, w2=w2,
                  bqkr=bqkr, b1c=b1c, bv=bv, bo=bo, br=brr, b2=b2,
                  identb=identb, ustrb=ustrb, maskb=maskb,
                  iotaf=iotaf, trashf=trashf)
    in_maps = [dict(common, x=np.ascontiguousarray(x[b])) for b in range(8)]
    res = run_bass_kernel_spmd(nc, in_maps, core_ids=list(range(8)))
    return np.stack([np.asarray(r["out"], dtype=np.float32) for r in res.results], axis=0)


# version 32 (final)
# speedup vs baseline: 1.0325x; 1.0325x over previous
import numpy as np
import ml_dtypes
import concourse.bass as bass
import concourse.mybir as mybir
from concourse.bass_utils import run_bass_kernel_spmd

F32 = mybir.dt.float32
BF16 = mybir.dt.bfloat16
U32 = mybir.dt.uint32
AF = mybir.ActivationFunctionType
ALU = mybir.AluOpType
AX = mybir.AxisListType
IOX = bass.IndirectOffsetOnAxis

S, D, NH, HD = 2048, 1024, 16, 64
HP = NH // 2
TC = S // 128
E, FH, CAP = 32, 256, 128
PERM_ROWS = E * CAP + S
NSLOT = 8
NEARLY = 1
EXP_SHIFT = -4.0
ATTN_SCALE = 0.125
LN_EPS = 1e-5
NEG_BIG = -1e30
DEBUG_DUMPS = False


class Ctr:
    def __init__(self, step):
        self.step = step
        self.v = 0


class Builder:
    def __init__(self, nc):
        self.nc = nc
        self.lists = {e: [] for e in ("sync", "scalar", "vector", "tensor", "gpsimd")}
        self.sems = {}
        self.ctrs = {}
        self._guards = []
        for name, step in [("sQ", 16), ("sG", 16), ("sS", 16),
                           ("sW0", 16), ("sW1", 16),
                           ("sPE", 1), ("sDV", 1), ("sAC", 1)]:
            g = nc.semaphore(name)
            self.sems[name] = g.__enter__()
            self._guards.append(g)
            self.ctrs[name] = Ctr(step)

    def W(self, eng, name, val):
        if val and val > 0:
            sem = self.sems[name]
            self.lists[eng].append(lambda En, s=sem, v=val: En.wait_ge(s, v))

    def OP(self, eng, fn, sem=None):
        if sem is not None:
            c = self.ctrs[sem]
            c.v += c.step
            val = c.v
            s = self.sems[sem]

            def wrapped(En, f=fn, s=s, st=c.step):
                f(En).then_inc(s, st)
            self.lists[eng].append(wrapped)
            return val
        self.lists[eng].append(lambda En, f=fn: f(En))
        return None

    def close(self):
        for g in self._guards:
            g.__exit__(None, None, None)


class Alloc:
    def __init__(self, nc):
        self.nc = nc
        self.live = {}

    def sb(self, name, shape, dt, side="left"):
        g = self.nc.sbuf_tensor(name, shape, dt, side=side)
        t = g.__enter__()
        self.live[name] = g
        return t

    def ps(self, name, shape, dt):
        g = self.nc.psum_tensor(name, shape, dt)
        t = g.__enter__()
        self.live[name] = g
        return t

    def free(self, *names):
        order = list(self.live)
        for n in sorted(names, key=order.index, reverse=True):
            self.live.pop(n).__exit__(None, None, None)

    def close(self):
        for n in reversed(list(self.live)):
            self.live.pop(n).__exit__(None, None, None)


def build_nc():
    nc = bass.Bass("TRN2", target_bir_lowering=False)

    dx = nc.dram_tensor("x", [S, D], F32, kind="ExternalInput").ap()
    dwqkv = nc.dram_tensor("wqkv", [D, 3 * D], F32, kind="ExternalInput").ap()
    dwout = nc.dram_tensor("wout", [D, D], F32, kind="ExternalInput").ap()
    dwr = nc.dram_tensor("wr", [E, D], F32, kind="ExternalInput").ap()
    dw1 = nc.dram_tensor("w1", [E, D, FH], F32, kind="ExternalInput").ap()
    dw2 = nc.dram_tensor("w2", [E, FH, D], F32, kind="ExternalInput").ap()
    dbqkr = nc.dram_tensor("bqkr", [1, 2 * D], F32, kind="ExternalInput").ap()
    db1c = nc.dram_tensor("b1c", [128, 2 * E], F32, kind="ExternalInput").ap()
    dbv = nc.dram_tensor("bv", [1, D], F32, kind="ExternalInput").ap()
    dbo = nc.dram_tensor("bo", [1, D], F32, kind="ExternalInput").ap()
    dbr = nc.dram_tensor("br", [1, E], F32, kind="ExternalInput").ap()
    db2 = nc.dram_tensor("b2", [E, D], F32, kind="ExternalInput").ap()
    dident = nc.dram_tensor("identb", [128, 128], BF16, kind="ExternalInput").ap()
    dust = nc.dram_tensor("ustrb", [128, 128], BF16, kind="ExternalInput").ap()
    dmask = nc.dram_tensor("maskb", [128, 2048], BF16, kind="ExternalInput").ap()
    diota = nc.dram_tensor("iotaf", [128, E], F32, kind="ExternalInput").ap()
    dtrash = nc.dram_tensor("trashf", [128, TC], F32, kind="ExternalInput").ap()

    dperm = nc.dram_tensor("perm_scr", [PERM_ROWS, D], BF16, kind="Internal").ap()
    dyperm = nc.dram_tensor("yperm_scr", [PERM_ROWS, D], BF16, kind="Internal").ap()
    drbscr = nc.dram_tensor("rb_scr", [4 * HP, D], F32, kind="Internal").ap()
    dout = nc.dram_tensor("out", [S, D], F32, kind="ExternalOutput").ap()

    al = Alloc(nc)
    bd = Builder(nc)
    W, OP = bd.W, bd.OP

    cID = al.sb("cID", [128, 128], BF16)
    cUST = al.sb("cUST", [128, 128], BF16)
    cMASK = al.sb("cMASK", [128, 2048], BF16)
    cIOTA = al.sb("cIOTA", [128, E], F32)
    cTRASH = al.sb("cTRASH", [128, TC], F32)
    b1c = al.sb("sb_b1c", [128, 2 * E], F32)
    onesb = al.sb("onesb", [1, 128], BF16)
    onesw = al.sb("onesw", [1, 512], BF16)
    expsh = al.sb("expsh", [128, 1], F32)
    epst = al.sb("epst", [128, 1], F32)
    onescol = al.sb("onescol", [128, 1], BF16)
    onesf = al.sb("onesf", [1, 128], F32)
    zer32 = al.sb("zer32", [128, E], F32)
    bvb = al.sb("bvb", [1, D], BF16)
    bob = al.sb("bob", [1, D], BF16)
    brb = al.sb("brb", [1, E], BF16)
    wrtb = al.sb("wrtb", [128, 8 * E], BF16)
    destu = al.sb("destu", [128, TC], U32)
    prefix = al.sb("prefix", [1, 2 * E], F32)
    stats = al.sb("stats", [128, 16], F32)
    lgs = al.sb("lgs", [128, E], F32)
    ohraw = al.sb("ohraw", [128, E], F32)
    csum = al.sb("csum", [128, E], F32)
    t1s = al.sb("t1s", [128, E], F32)
    ohs = al.sb("ohs", [128, E], F32)
    ohb = al.sb("ohb", [128, E], BF16)
    rks = al.sb("rks", [128, E], F32)
    rkm = al.sb("rkm", [128, E], F32)
    cnts = al.sb("cnts", [1, E], F32)
    scal = al.sb("scal", [128, 8], F32)
    b2s = [al.sb(f"b2s{i}", [1, D], BF16) for i in range(2)]
    w1s, w2s = {}, {}
    for i in range(NEARLY):
        w1s[i] = al.sb(f"w1s{i}", [128, 8 * FH], BF16)
        w2s[i] = al.sb(f"w2s{i}", [128, 2 * D], BF16)
    xfm = al.sb("xfm", [128, TC * D], BF16)

    consts_v = 0
    for dst, src in [(cID, dident), (cUST, dust), (cMASK, dmask), (cIOTA, diota),
                     (cTRASH, dtrash), (b1c, db1c)]:
        consts_v = OP("sync", lambda En, d=dst, s=src: En.dma_start(out=d[:], in_=s), "sQ")

    bvb_v = OP("gpsimd", lambda En: En.dma_start(out=bvb[:], in_=dbv), "sG")
    OP("gpsimd", lambda En: En.dma_start(out=bob[:], in_=dbo), "sG")
    brb_v = OP("gpsimd", lambda En: En.dma_start(out=brb[:], in_=dbr), "sG")

    OP("vector", lambda En: En.memset(onesb[:], 1.0))
    OP("vector", lambda En: En.memset(onesw[:], 1.0))
    OP("vector", lambda En: En.memset(expsh[:], EXP_SHIFT))
    OP("vector", lambda En: En.memset(epst[:], LN_EPS))
    OP("vector", lambda En: En.memset(onescol[:], 1.0))
    OP("vector", lambda En: En.memset(onesf[:], 1.0))
    OP("vector", lambda En: En.memset(zer32[:], 0.0))
    ms_v = OP("vector", lambda En: En.memset(prefix[:], 0.0), "sDV")

    wrstg = al.sb("wrstg", [E, D], BF16, side="right")
    xstg = [al.sb(f"xstg{i}", [128, D], F32, side="right") for i in range(2)]
    xbstg = [al.sb(f"xbstg{i}", [128, D], BF16, side="right") for i in range(2)]

    wrstg_v = OP("gpsimd", lambda En: En.dma_start(out=wrstg[:], in_=dwr), "sG")

    pXt = [al.ps(f"pXt{i}", [128, 1024], BF16) for i in range(2)]
    xr = dx.rearrange("(c p) d -> c p d", p=128)
    xld, xcast, xtr, xev = [0] * TC, [0] * TC, [0] * TC, [0] * TC
    for i in range(TC):
        if i >= 2:
            W("sync", "sDV", xcast[i - 2])
        xld[i] = OP("sync", lambda En, i=i: En.dma_start(out=xstg[i % 2][:], in_=xr[i]), "sQ")
        W("vector", "sQ", xld[i])
        if i >= 2:
            W("vector", "sPE", xtr[i - 2])
        xcast[i] = OP("vector", lambda En, i=i: En.tensor_copy(xbstg[i % 2][:], xstg[i % 2][:]), "sDV")
        W("tensor", "sDV", xcast[i])
        if i >= 2:
            W("tensor", "sDV", xev[i - 2])
        for j in range(8):
            xtr[i] = OP("tensor", lambda En, i=i, j=j: En.transpose(
                pXt[i % 2][:, 128 * j:128 * (j + 1)],
                xbstg[i % 2][:, 128 * j:128 * (j + 1)], cID[:]), "sPE")
        W("vector", "sPE", xtr[i])
        xev[i] = OP("vector", lambda En, i=i: En.tensor_copy(
            bass.AP(xfm[:].tensor, xfm[:].offset + 128 * i,
                    [xfm[:].ap[0], [2048, 8], [1, 128]]),
            pXt[i % 2][:].rearrange("p (j c) -> p j c", c=128)), "sDV")
    xfm_v = xev[TC - 1]

    w1r = dw1.rearrange("e (k p) h -> e p k h", p=128)
    w2r = dw2.rearrange("e (k p) m -> e p k m", p=128)
    slot_gate = {}
    exp_loaded = {}

    b2_loaded = {}

    def emit_expert_load(e, extra_pe_gate=None, extra_dv_gate=None):
        s = e % NSLOT
        g = slot_gate.get(s)
        if g:
            W("gpsimd", "sPE", g)
        if extra_pe_gate:
            W("gpsimd", "sPE", extra_pe_gate)
        if extra_dv_gate:
            W("gpsimd", "sDV", extra_dv_gate)
        OP("gpsimd", lambda En, e=e, s=s: En.dma_start(out=w1s[s][:], in_=w1r[e]), "sG")
        exp_loaded[e] = OP("gpsimd", lambda En, e=e, s=s: En.dma_start(
            out=w2s[s][:], in_=w2r[e]), "sG")

    def emit_b2_load(e, pe_gate=None):
        if pe_gate:
            W("gpsimd", "sPE", pe_gate)
        b2_loaded[e] = OP("gpsimd", lambda En, e=e: En.dma_start(
            out=b2s[e % 2][:], in_=db2[e:e + 1, :]), "sG")

    pRtr = al.ps("pRtr", [128, 8 * E], BF16)
    W("tensor", "sG", wrstg_v)
    W("tensor", "sQ", consts_v)
    wrt_pe = 0
    for k in range(8):
        wrt_pe = OP("tensor", lambda En, k=k: En.transpose(
            pRtr[:, E * k:E * (k + 1)], wrstg[:, 128 * k:128 * (k + 1)], cID[0:E, 0:E]), "sPE")
    W("vector", "sPE", wrt_pe)
    wrtb_v = OP("vector", lambda En: En.tensor_copy(wrtb[:], pRtr[:]), "sDV")

    for e in range(NEARLY):
        emit_expert_load(e)
    for e in range(2):
        emit_b2_load(e)

    al.free("xbstg1", "xbstg0", "xstg1", "xstg0", "wrstg", "pRtr", "pXt0", "pXt1")

    qfm = al.sb("qfm", [128, TC * D], BF16, side="right")
    kfm = al.sb("kfm", [128, TC * D], BF16, side="right")
    vtm = al.sb("vtm", [128, TC * NH * 65], BF16, side="right")
    bqkrb = al.sb("bqkrb", [1, 2 * D], BF16, side="right")
    wvb = al.sb("wvb", [128, 8 * D], BF16, side="right")
    wqkb = al.sb("wqkb", [128, 8 * 2 * D], BF16, side="right")

    vones_v = OP("vector", lambda En: En.memset(vtm[:, 64:TC * NH * 65:65], 1.0), "sDV")

    wvr = dwqkv[:, 2 * D:3 * D].rearrange("(k p) m -> k p m", p=128)
    wv_v = 0
    for k in range(8):
        wv_v = OP("gpsimd", lambda En, k=k: En.dma_start(
            out=wvb[:, D * k:D * (k + 1)], in_=wvr[k]), "sG")
    bqkr_v = OP("gpsimd", lambda En: En.dma_start(out=bqkrb[:], in_=dbqkr), "sG")
    wqkr = dwqkv[:, 0:2 * D].rearrange("(k p) m -> k p m", p=128)
    wqk_v = [0] * 8
    for k in range(8):
        wqk_v[k] = OP("gpsimd", lambda En, k=k: En.dma_start(
            out=wqkb[:, 2048 * k:2048 * (k + 1)], in_=wqkr[k]), "sG")

    pQK = al.ps("pQK", [128, 2048], F32)
    pV = al.ps("pV", [128, 1024], F32)

    W("tensor", "sDV", xfm_v)
    W("tensor", "sG", wv_v)
    W("tensor", "sG", bvb_v)
    W("tensor", "sG", bqkr_v)
    W("tensor", "sDV", ms_v)
    W("tensor", "sDV", wrtb_v)

    slot_user = {}
    v_done_v = 0
    for tc in range(TC):
        for n2 in range(2):
            slot = (tc * 2 + n2) % 2
            g = slot_user.get(("v", slot))
            if g:
                W("tensor", "sDV", g)
            for k in range(8):
                OP("tensor", lambda En, k=k, tc=tc, n2=n2, slot=slot: En.matmul(
                    pV[:, 512 * slot:512 * (slot + 1)],
                    xfm[:, 2048 * k + 128 * tc:2048 * k + 128 * (tc + 1)],
                    wvb[:, D * k + 512 * n2:D * k + 512 * (n2 + 1)],
                    start=(k == 0), stop=False), "sPE")
            pe_v = OP("tensor", lambda En, n2=n2, slot=slot: En.matmul(
                pV[:, 512 * slot:512 * (slot + 1)],
                onesb[:], bvb[0:1, 512 * n2:512 * (n2 + 1)],
                start=False, stop=True), "sPE")
            W("vector", "sPE", pe_v)
            dv_v = OP("vector", lambda En, tc=tc, n2=n2, slot=slot: En.tensor_copy(
                vtm[:, 1040 * tc + 520 * n2:1040 * tc + 520 * (n2 + 1)]
                    .rearrange("p (h c) -> p h c", c=65)[:, :, 0:64],
                pV[:, 512 * slot:512 * (slot + 1)].rearrange("p (h c) -> p h c", c=64)), "sDV")
            slot_user[("v", slot)] = dv_v
            v_done_v = dv_v

    qk_done_v = 0
    for hp in range(HP):
        for which in (0, 1):
            for n in range(4):
                slot = (hp * 4 + n) % 2 + 2 * which
                g = slot_user.get(("qk", slot))
                if g:
                    W("tensor", "sAC", g)
                pe_v = 0
                for k in range(8):
                    W("tensor", "sG", wqk_v[k])
                    ofc = 2048 * k + 1024 * which + 128 * hp
                    pe_v = OP("tensor", lambda En, k=k, ofc=ofc, n=n, slot=slot: En.matmul(
                        pQK[:, 512 * slot:512 * (slot + 1)],
                        wqkb[:, ofc:ofc + 128],
                        xfm[:, 2048 * k + 512 * n:2048 * k + 512 * (n + 1)],
                        start=(k == 0), stop=False), "sPE")
                boff = 1024 * which + 128 * hp
                pe_v = OP("tensor", lambda En, boff=boff, slot=slot: En.matmul(
                    pQK[:, 512 * slot:512 * (slot + 1)],
                    bqkrb[0:1, boff:boff + 128], onesw[:],
                    start=False, stop=True), "sPE")
                dst = qfm if which == 0 else kfm
                W("scalar", "sPE", pe_v)
                ac_v = OP("scalar", lambda En, dst=dst, hp=hp, n=n, slot=slot: En.copy(
                    dst[:, 2048 * hp + 512 * n:2048 * hp + 512 * (n + 1)],
                    pQK[:, 512 * slot:512 * (slot + 1)]), "sAC")
                slot_user[("qk", slot)] = ac_v
                qk_done_v = ac_v

    al.free("wqkb", "wvb", "bqkrb", "pQK", "pV")

    rr = al.sb("rr", [128, 1024], F32, side="right")
    rb = al.sb("rb", [128, 2048], F32, side="right")
    ring = al.sb("ring", [128, 4096], BF16, side="right")
    psc = al.ps("psc", [128, 2048], F32)
    poAs = [al.ps(f"poA{i}", [65, 512], F32) for i in range(2)]
    poBs = [al.ps(f"poB{i}", [65, 512], F32) for i in range(2)]
    attnout = xfm

    W("tensor", "sAC", qk_done_v)
    W("tensor", "sDV", v_done_v)
    W("tensor", "sDV", vones_v)
    W("scalar", "sDV", ms_v)

    slot_exp = {}
    ring_pv = {}
    o_mult_done = {0: 0, 1: 0}
    rr_store_last = 0
    attn_done_v = 0
    itn = 0

    for qc in range(4):
        kv0 = 4 * qc
        for hp in range(HP):
            par = itn % 2
            poA, poB = poAs[par], poBs[par]
            sc_v, exp_v, pv_v = {}, {}, {}

            def emit_scores(kvc, qc=qc, hp=hp, sc_v=None):
                slot = kvc % 2
                if slot_exp.get(slot):
                    W("tensor", "sAC", slot_exp[slot])
                diag = kvc < 4 * qc + 4
                o = kvc - 4 * qc
                last = None
                for sub in (0, 1):
                    base = 64 * sub
                    col = 1024 * slot + 512 * sub
                    last = OP("tensor", lambda En, kvc=kvc, col=col, base=base, diag=diag, hp=hp, qc=qc: En.matmul(
                        psc[:, col:col + 512],
                        kfm[base:base + 64, 2048 * hp + 128 * kvc:2048 * hp + 128 * (kvc + 1)],
                        qfm[base:base + 64, 2048 * hp + 512 * qc:2048 * hp + 512 * (qc + 1)],
                        start=True, stop=(not diag), tile_position=(base, 0)), "sPE")
                if diag:
                    for sub in (0, 1):
                        col = 1024 * slot + 512 * sub
                        last = OP("tensor", lambda En, col=col, o=o: En.matmul(
                            psc[:, col:col + 512], cID[:],
                            cMASK[:, 512 * o:512 * (o + 1)],
                            start=False, stop=True), "sPE")
                sc_v[kvc] = last

            def emit_exp(kvc, sc_v=None, exp_v=None):
                slot = kvc % 2
                rslot = kvc % 4
                W("scalar", "sPE", sc_v[kvc])
                if ring_pv.get(rslot):
                    W("scalar", "sPE", ring_pv[rslot])
                exp_v[kvc] = OP("scalar", lambda En, slot=slot, rslot=rslot: En.activation(
                    ring[:, 1024 * rslot:1024 * (rslot + 1)],
                    psc[:, 1024 * slot:1024 * (slot + 1)],
                    AF.Exp, bias=expsh[:], scale=ATTN_SCALE), "sAC")
                slot_exp[slot] = exp_v[kvc]

            def emit_pv(kvc, qc=qc, hp=hp, kv0=kv0, par=par, poA=poA, poB=poB, exp_v=None, pv_v=None):
                rslot = kvc % 4
                W("tensor", "sAC", exp_v[kvc])
                if kvc == kv0 and o_mult_done[par]:
                    W("tensor", "sDV", o_mult_done[par])
                OP("tensor", lambda En, kvc=kvc, rslot=rslot, hp=hp, kv0=kv0: En.matmul(
                    poA[:], vtm[:, 1040 * kvc + 65 * (2 * hp):1040 * kvc + 65 * (2 * hp) + 65],
                    ring[:, 1024 * rslot:1024 * rslot + 512],
                    start=(kvc == kv0), stop=(kvc == TC - 1)), "sPE")
                pv_v[kvc] = OP("tensor", lambda En, kvc=kvc, rslot=rslot, hp=hp, kv0=kv0: En.matmul(
                    poB[:], vtm[:, 1040 * kvc + 65 * (2 * hp + 1):1040 * kvc + 65 * (2 * hp + 1) + 65],
                    ring[:, 1024 * rslot + 512:1024 * (rslot + 1)],
                    start=(kvc == kv0), stop=(kvc == TC - 1)), "sPE")
                ring_pv[rslot] = pv_v[kvc]

            for kvc in range(kv0, TC):
                emit_scores(kvc, sc_v=sc_v)
                emit_exp(kvc, sc_v=sc_v, exp_v=exp_v)
                if kvc > kv0:
                    emit_pv(kvc - 1, exp_v=exp_v, pv_v=pv_v)
            emit_pv(TC - 1, exp_v=exp_v, pv_v=pv_v)
            pv_all = pv_v[TC - 1]

            W("vector", "sPE", pv_all)
            if rr_store_last:
                W("vector", "sQ", rr_store_last)
            OP("vector", lambda En, poA=poA: En.reciprocal(rr[64:65, 0:512], poA[64:65, :]))
            rc1 = OP("vector", lambda En, poB=poB: En.reciprocal(rr[64:65, 512:1024], poB[64:65, :]), "sDV")

            W("sync", "sDV", rc1)
            st_rr = OP("sync", lambda En, itn=itn: En.dma_start(
                out=drbscr[itn:itn + 1, :], in_=rr[64:65, 0:1024]), "sQ")
            rr_store_last = st_rr
            W("sync", "sQ", st_rr)
            if o_mult_done[par]:
                W("sync", "sDV", o_mult_done[par])
            base_ap = drbscr[itn:itn + 1, :]
            bcast_v = OP("sync", lambda En, itn=itn, par=par, base_ap=base_ap: En.dma_start(
                out=rb[0:64, 1024 * par:1024 * (par + 1)],
                in_=bass.AP(base_ap.tensor, base_ap.offset, [[0, 64], [1, 1024]])), "sQ")

            W("vector", "sQ", bcast_v)
            OP("vector", lambda En, hp=hp, qc=qc, par=par, poA=poA: En.tensor_tensor(
                attnout[0:64, 2048 * hp + 512 * qc:2048 * hp + 512 * (qc + 1)],
                poA[0:64, :], rb[0:64, 1024 * par:1024 * par + 512], ALU.mult))
            o_mult_done[par] = OP("vector", lambda En, hp=hp, qc=qc, par=par, poB=poB: En.tensor_tensor(
                attnout[64:128, 2048 * hp + 512 * qc:2048 * hp + 512 * (qc + 1)],
                poB[0:64, :], rb[0:64, 1024 * par + 512:1024 * (par + 1)], ALU.mult), "sDV")
            attn_done_v = o_mult_done[par]
            itn += 1

    attn_pe_v = bd.ctrs["sPE"].v
    al.free("psc", "poA0", "poA1", "poB0", "poB1", "ring", "rb", "rr", "vtm", "kfm", "qfm")

    woutb = al.sb("woutb", [128, 8 * D], BF16, side="right")
    x1fm = [al.sb(f"x1fm{i}", [128, D], BF16, side="right") for i in range(2)]
    x1bf = al.sb("x1bf", [128, TC * D], BF16, side="right")
    sqscr = al.sb("sqscr", [128, D], F32, side="right")
    x1pre = [al.sb(f"x1pre{i}", [128, D], F32, side="right") for i in range(2)]
    x2pre = al.sb("x2pre", [128, D], F32, side="right")
    xres = [al.sb(f"xres{i}", [128, D], F32, side="right") for i in range(2)]
    outstg = [al.sb(f"outstg{i}", [128, D], F32, side="right") for i in range(2)]
    ygat = [al.sb(f"ygat{i}", [128, D], BF16, side="right") for i in range(2)]
    xet = [al.sb(f"xet{i}", [128, D], BF16, side="right") for i in range(3)]
    hbf = [al.sb(f"hbf{i}", [128, FH], BF16, side="right") for i in range(2)]
    for i in range(NEARLY, 7):
        w1s[i] = al.sb(f"w1s{i}", [128, 8 * FH], BF16, side="right")
        w2s[i] = al.sb(f"w2s{i}", [128, 2 * D], BF16, side="right")

    pOut = [al.ps(f"pOut{i}", [128, 1024], F32) for i in range(2)]
    pTr = al.ps("pTr", [128, 1024], BF16)
    pRt = [al.ps(f"pRt{i}", [128, 512], F32) for i in range(2)]

    wor = dwout.rearrange("(k p) m -> k p m", p=128)
    wout_k = [0] * 8
    W("gpsimd", "sPE", attn_pe_v)
    for k in range(8):
        wout_k[k] = OP("gpsimd", lambda En, k=k: En.dma_start(
            out=woutb[:, D * k:D * (k + 1)], in_=wor[k]), "sG")

    for e in range(NEARLY, 7):
        emit_expert_load(e, extra_pe_gate=attn_pe_v, extra_dv_gate=attn_done_v)

    def emit_ln(src, src_eng, src_v, dst_fn):
        W("vector", src_eng, src_v)
        s1 = OP("vector", lambda En: En.reduce_sum(
            stats[:, 0:1], src[:], axis=AX.X, negate=True), "sDV")
        W("scalar", src_eng, src_v)
        s2 = OP("scalar", lambda En: En.activation(
            sqscr[:], src[:], AF.Square, accum_out=stats[:, 1:2]), "sAC")
        W("vector", "sDV", s1)
        W("vector", "sAC", s2)
        OP("vector", lambda En: En.tensor_scalar(
            stats[:, 2:3], stats[:, 0:1], 1.0 / D, None, ALU.mult))
        s3 = OP("vector", lambda En: En.tensor_scalar(
            stats[:, 3:4], stats[:, 1:2], 1.0 / D, None, ALU.mult), "sDV")
        W("vector", "sDV", s3)
        s4 = OP("vector", lambda En: En.tensor_tensor(
            stats[:, 4:5], stats[:, 2:3], stats[:, 2:3], ALU.mult), "sDV")
        W("vector", "sDV", s4)
        s5 = OP("vector", lambda En: En.tensor_tensor(
            stats[:, 5:6], stats[:, 3:4], stats[:, 4:5], ALU.subtract), "sDV")
        W("scalar", "sDV", s5)
        s6 = OP("scalar", lambda En: En.activation(
            stats[:, 6:7], stats[:, 5:6], AF.Sqrt, bias=epst[:]), "sAC")
        W("vector", "sAC", s6)
        s7 = OP("vector", lambda En: En.reciprocal(stats[:, 7:8], stats[:, 6:7]), "sDV")
        W("vector", "sDV", s7)
        return dst_fn()

    if DEBUG_DUMPS:
        dbgpre = nc.dram_tensor("dbg_x1pre", [S, D], F32, kind="ExternalOutput").ap()
        dbgprer = dbgpre.rearrange("(c p) d -> c p d", p=128)
    outp_free = {0: attn_done_v, 1: attn_done_v}
    pTr_free = 0
    pRt_free = {0: attn_done_v, 1: attn_done_v}
    x1fm_free = {0: 0, 1: 0}
    x1pre_free = {0: 0, 1: 0}
    xres_free = {0: 0, 1: 0}
    x1bf_v = [0] * TC
    dest_done = [0] * TC
    scat_done = [0] * TC
    prev_dest_chain = 0
    prev_cnt_pe = 0
    lg_pe = 0

    def p3_stage1(tc):
        if tc < 2:
            W("sync", "sDV", attn_done_v)
        if xres_free[tc % 2]:
            W("sync", "sDV", xres_free[tc % 2])
        xrld = OP("sync", lambda En, tc=tc: En.dma_start(out=xres[tc % 2][:], in_=xr[tc]), "sQ")

        resadd_v = 0
        for n2 in range(2):
            W("tensor", "sDV", outp_free[tc % 2])
            for k in range(8):
                if tc == 0:
                    W("tensor", "sG", wout_k[k])
                OP("tensor", lambda En, k=k, tc=tc, n2=n2: En.matmul(
                    pOut[tc % 2][:, 512 * n2:512 * (n2 + 1)],
                    attnout[:, 2048 * k + 128 * tc:2048 * k + 128 * (tc + 1)],
                    woutb[:, 1024 * k + 512 * n2:1024 * k + 512 * (n2 + 1)],
                    start=(k == 0), stop=False), "sPE")
            pe_v = OP("tensor", lambda En, tc=tc, n2=n2: En.matmul(
                pOut[tc % 2][:, 512 * n2:512 * (n2 + 1)],
                onesb[:], bob[0:1, 512 * n2:512 * (n2 + 1)],
                start=False, stop=True), "sPE")
            W("vector", "sPE", pe_v)
            W("vector", "sQ", xrld)
            if n2 == 0 and x1pre_free[tc % 2]:
                W("vector", "sDV", x1pre_free[tc % 2])
            resadd_v = OP("vector", lambda En, tc=tc, n2=n2: En.tensor_tensor(
                x1pre[tc % 2][:, 512 * n2:512 * (n2 + 1)],
                pOut[tc % 2][:, 512 * n2:512 * (n2 + 1)],
                xres[tc % 2][:, 512 * n2:512 * (n2 + 1)], ALU.add), "sDV")
        outp_free[tc % 2] = resadd_v
        xres_free[tc % 2] = resadd_v

        if DEBUG_DUMPS:
            W("sync", "sDV", resadd_v)
            dbg_v = OP("sync", lambda En, tc=tc: En.dma_start(
                out=dbgprer[tc], in_=x1pre[tc % 2][:]), "sQ")
            W("vector", "sQ", dbg_v)

        def ln1_dst(tc=tc):
            return OP("vector", lambda En: En.tensor_scalar(
                x1bf[:, D * tc:D * (tc + 1)], x1pre[tc % 2][:],
                stats[:, 2:3], stats[:, 7:8], ALU.add, ALU.mult), "sDV")
        x1bf_v[tc] = emit_ln(x1pre[tc % 2], "sDV", resadd_v, ln1_dst)
        x1pre_free[tc % 2] = x1bf_v[tc]

    def p3_stage2(tc):
        nonlocal pTr_free, prev_dest_chain, prev_cnt_pe, lg_pe
        W("tensor", "sDV", x1bf_v[tc])
        if pTr_free:
            W("tensor", "sDV", pTr_free)
        tr_v = 0
        for j in range(8):
            tr_v = OP("tensor", lambda En, tc=tc, j=j: En.transpose(
                pTr[:, 128 * j:128 * (j + 1)],
                x1bf[:, D * tc + 128 * j:D * tc + 128 * (j + 1)], cID[:]), "sPE")
        W("vector", "sPE", tr_v)
        if x1fm_free[tc % 2]:
            W("vector", "sPE", x1fm_free[tc % 2])
        x1fm_v = OP("vector", lambda En, tc=tc: En.tensor_copy(x1fm[tc % 2][:], pTr[:]), "sDV")
        pTr_free = x1fm_v

        W("tensor", "sDV", x1fm_v)
        W("tensor", "sDV", pRt_free[tc % 2])
        W("tensor", "sG", brb_v)
        for k in range(8):
            OP("tensor", lambda En, k=k, tc=tc: En.matmul(
                pRt[tc % 2][:, 0:E],
                x1fm[tc % 2][:, 128 * k:128 * (k + 1)],
                wrtb[:, E * k:E * (k + 1)],
                start=(k == 0), stop=False), "sPE")
        lg_pe = OP("tensor", lambda En, tc=tc: En.matmul(
            pRt[tc % 2][:, 0:E], onesb[:], brb[:], start=False, stop=True), "sPE")
        x1fm_free[tc % 2] = lg_pe

        W("vector", "sPE", lg_pe)
        if prev_dest_chain:
            W("vector", "sDV", prev_dest_chain)
        c1 = OP("vector", lambda En, tc=tc: En.tensor_copy(lgs[:], pRt[tc % 2][:, 0:E]), "sDV")
        W("vector", "sDV", c1)
        c2 = OP("vector", lambda En: En.reduce_max(stats[:, 8:9], lgs[:], axis=AX.X), "sDV")
        W("vector", "sDV", c2)
        c3 = OP("vector", lambda En: En.tensor_scalar(
            ohraw[:], lgs[:], stats[:, 8:9], None, ALU.is_equal), "sDV")
        W("vector", "sDV", c3)
        c4 = OP("vector", lambda En: En.tensor_tensor_scan(
            csum[:], ohraw[:], zer32[:], 0.0, ALU.add, ALU.add), "sDV")
        W("vector", "sDV", c4)
        c6 = OP("vector", lambda En: En.scalar_tensor_tensor(
            ohs[:], csum[:], 1.0, ohraw[:], ALU.is_equal, ALU.mult), "sDV")
        W("vector", "sDV", c6)
        if prev_cnt_pe:
            W("vector", "sPE", prev_cnt_pe)
        ohb_v = OP("vector", lambda En: En.tensor_copy(ohb[:], ohs[:]), "sDV")

        W("tensor", "sDV", ohb_v)
        OP("tensor", lambda En, tc=tc: En.matmul(
            pRt[tc % 2][:, 64:64 + E], cUST[:], ohb[:], start=True, stop=False), "sPE")
        OP("tensor", lambda En, tc=tc: En.matmul(
            pRt[tc % 2][:, 64:64 + E], onesf[:],
            prefix[0:1, E * (tc % 2):E * (tc % 2) + E],
            start=False, stop=True), "sPE")
        cnt_pe = OP("tensor", lambda En, tc=tc: En.matmul(
            pRt[tc % 2][0:1, 128:128 + E], onescol[:], ohb[:], start=True, stop=True), "sPE")
        prev_cnt_pe = cnt_pe

        W("vector", "sPE", cnt_pe)
        d1 = OP("vector", lambda En, tc=tc: En.tensor_copy(rks[:], pRt[tc % 2][:, 64:64 + E]), "sDV")
        d2 = OP("vector", lambda En, tc=tc: En.tensor_copy(cnts[:], pRt[tc % 2][0:1, 128:128 + E]), "sDV")
        pRt_free[tc % 2] = d2
        W("vector", "sDV", d2)
        OP("vector", lambda En, tc=tc: En.tensor_tensor(
            prefix[0:1, E * ((tc + 1) % 2):E * ((tc + 1) % 2) + E],
            prefix[0:1, E * (tc % 2):E * (tc % 2) + E], cnts[:], ALU.add))
        OP("vector", lambda En: En.tensor_tensor(rkm[:], rks[:], ohs[:], ALU.mult))
        d3 = OP("vector", lambda En: En.tensor_tensor(t1s[:], cIOTA[:], ohs[:], ALU.mult), "sDV")
        W("vector", "sDV", d3)
        OP("vector", lambda En: En.reduce_sum(scal[:, 0:1], rkm[:], axis=AX.X))
        d4 = OP("vector", lambda En: En.reduce_sum(scal[:, 1:2], t1s[:], axis=AX.X), "sDV")
        W("vector", "sDV", d4)
        OP("vector", lambda En: En.tensor_scalar(
            scal[:, 2:3], scal[:, 1:2], 128.0, scal[:, 0:1], ALU.mult, ALU.add))
        d5 = OP("vector", lambda En: En.tensor_scalar(
            scal[:, 3:4], scal[:, 0:1], 127.5, None, ALU.is_gt), "sDV")
        W("vector", "sDV", d5)
        d6 = OP("vector", lambda En, tc=tc: En.tensor_tensor(
            scal[:, 4:5], cTRASH[:, tc:tc + 1], scal[:, 2:3], ALU.subtract), "sDV")
        W("vector", "sDV", d6)
        d8 = OP("vector", lambda En: En.scalar_tensor_tensor(
            scal[:, 6:7], scal[:, 4:5], scal[:, 3:4], scal[:, 2:3],
            ALU.mult, ALU.add), "sDV")
        W("vector", "sDV", d8)
        dest_done[tc] = OP("vector", lambda En, tc=tc: En.tensor_copy(
            destu[:, tc:tc + 1], scal[:, 6:7]), "sDV")
        prev_dest_chain = dest_done[tc]

        W("gpsimd", "sDV", dest_done[tc])
        scat_done[tc] = OP("gpsimd", lambda En, tc=tc: En.indirect_dma_start(
            out=dperm, out_offset=IOX(destu[:, tc:tc + 1], 0),
            in_=x1bf[:, D * tc:D * (tc + 1)], in_offset=None), "sG")

    for tc in range(TC + 2):
        if tc < TC:
            p3_stage1(tc)
        if tc >= 2:
            p3_stage2(tc - 2)

    p3_dve_last = dest_done[TC - 1]
    p3_pe_last = bd.ctrs["sPE"].v
    al.free("pOut0", "pOut1", "pTr", "pRt0", "pRt1")
    al.free("xfm")
    for i in range(7, NSLOT):
        w1s[i] = al.sb(f"w1s{i}", [128, 8 * FH], BF16)
        w2s[i] = al.sb(f"w2s{i}", [128, 2 * D], BF16)
    wstg = [al.sb(f"wstg{i}", [128, 2048], F32) for i in range(3)]
    slot_gate[7] = p3_pe_last

    w2cast = {}
    w2stage = {}
    wstg_q = {}

    def emit_w1_only(e, extra_pe_gate=None, extra_dv_gate=None):
        s = e % NSLOT
        g = slot_gate.get(s)
        if g:
            W("gpsimd", "sPE", g)
        if extra_pe_gate:
            W("gpsimd", "sPE", extra_pe_gate)
        if extra_dv_gate:
            W("gpsimd", "sDV", extra_dv_gate)
        exp_loaded[e] = OP("gpsimd", lambda En, e=e, s=s: En.dma_start(
            out=w1s[s][:], in_=w1r[e]), "sG")

    def emit_w2_stage(e, pe_gate=None):
        if pe_gate:
            W("scalar", "sPE", pe_gate)
        if e >= 10:
            W("scalar", "sAC", w2cast[e - 3])
        w2stage[e] = OP("scalar", lambda En, e=e: En.dma_start(
            out=wstg[e % 3][:], in_=w2r[e]), "sW1")

    def emit_w2_cast(e):
        s = e % NSLOT
        W("scalar", "sW1", w2stage[e])
        g = slot_gate.get(s)
        if g:
            W("scalar", "sPE", g)
        w2cast[e] = OP("scalar", lambda En, e=e, s=s: En.copy(
            w2s[s][:], wstg[e % 3][:]), "sAC")

    emit_w1_only(7, extra_pe_gate=p3_pe_last)
    emit_w2_stage(7, pe_gate=p3_pe_last)
    emit_w2_stage(8, pe_gate=p3_pe_last)

    pXe = [al.ps(f"pXe{i}", [128, 1024], BF16) for i in range(2)]
    pH = [al.ps(f"pH{i}", [128, 128], F32) for i in range(2)]
    pY = [al.ps(f"pY{i}", [128, 1024], F32) for i in range(2)]
    xef = x1fm
    ybf = ygat

    permr = dperm.rearrange("(e p) d -> e p d", p=128)
    ypermr = dyperm.rearrange("(e p) d -> e p d", p=128)

    xet_free = {0: 0, 1: 0, 2: 0}
    pXe_free = {0: 0, 1: 0}
    xef_free = {0: lg_pe, 1: lg_pe}
    xef_v = {0: 0, 1: 0}
    pH_free = {0: 0, 1: 0}
    pY_free = {0: 0, 1: 0}
    hbf_free = {0: 0, 1: 0}
    ybf_free = {0: 0, 1: 0}
    ystore = [0] * E

    W("tensor", "sDV", p3_dve_last)
    W("sync", "sG", scat_done[TC - 1])

    xe_ld = {}

    def emit_xe_load(e):
        sx = e % 3
        if xet_free[sx]:
            W("sync", "sPE", xet_free[sx])
        xe_ld[e] = OP("sync", lambda En, e=e, sx=sx: En.dma_start(
            out=xet[sx][:], in_=permr[e]), "sQ")

    emit_xe_load(0)
    emit_xe_load(1)

    def p4_front(e):
        sl = e % 2
        sx = e % 3
        if e + 2 < E:
            emit_xe_load(e + 2)
        W("tensor", "sQ", xe_ld[e])
        if pXe_free[sl]:
            W("tensor", "sDV", pXe_free[sl])
        tr_v = 0
        for j in range(8):
            tr_v = OP("tensor", lambda En, sl=sl, sx=sx, j=j: En.transpose(
                pXe[sl][:, 128 * j:128 * (j + 1)], xet[sx][:, 128 * j:128 * (j + 1)], cID[:]), "sPE")
        xet_free[sx] = tr_v
        W("vector", "sPE", tr_v)
        if xef_free[sl]:
            W("vector", "sPE", xef_free[sl])
        xef_v[sl] = OP("vector", lambda En, sl=sl: En.tensor_copy(xef[sl][:], pXe[sl][:]), "sDV")
        pXe_free[sl] = xef_v[sl]

    def p4_back(e):
        sl = e % 2
        W("tensor", "sG", exp_loaded[e])
        W("tensor", "sDV", xef_v[sl])
        h_ac = {}
        h_pe_last = 0
        for hc in range(2):
            if pH_free[hc]:
                W("tensor", "sAC", pH_free[hc])
            h_pe = 0
            for k in range(8):
                h_pe = OP("tensor", lambda En, sl=sl, e=e, hc=hc, k=k: En.matmul(
                    pH[hc][:],
                    w1s[e % NSLOT][:, FH * k + 128 * hc:FH * k + 128 * (hc + 1)],
                    xef[sl][:, 128 * k:128 * (k + 1)],
                    start=(k == 0), stop=(k == 7)), "sPE")
            h_pe_last = h_pe
            W("scalar", "sPE", h_pe)
            if hbf_free[sl]:
                W("scalar", "sPE", hbf_free[sl])
            hb_v = OP("scalar", lambda En, sl=sl, e=e, hc=hc: En.activation(
                hbf[sl][:, 128 * hc:128 * (hc + 1)], pH[hc][:],
                AF.Relu, bias=b1c[:, 2 * e + hc:2 * e + hc + 1]), "sAC")
            pH_free[hc] = hb_v
            h_ac[hc] = OP("scalar", lambda En: En.copy(
                sqscr[:, 0:1], epst[:]), "sAC")
        xef_free[sl] = h_pe_last

        if 7 <= e + 7 < E:
            emit_w2_cast(e + 7)

        W("tensor", "sG", b2_loaded[e])
        if e >= 7:
            W("tensor", "sAC", w2cast[e])
        y_pe = 0
        for n2 in range(2):
            if pY_free[sl] and n2 == 0:
                W("tensor", "sAC", pY_free[sl])
            for hc in range(2):
                if n2 == 0:
                    W("tensor", "sAC", h_ac[hc])
                OP("tensor", lambda En, sl=sl, e=e, n2=n2, hc=hc: En.matmul(
                    pY[sl][:, 512 * n2:512 * (n2 + 1)],
                    hbf[sl][:, 128 * hc:128 * (hc + 1)],
                    w2s[e % NSLOT][:, D * hc + 512 * n2:D * hc + 512 * (n2 + 1)],
                    start=(hc == 0), stop=False), "sPE")
            y_pe = OP("tensor", lambda En, sl=sl, e=e, n2=n2: En.matmul(
                pY[sl][:, 512 * n2:512 * (n2 + 1)],
                onesb[:], b2s[e % 2][0:1, 512 * n2:512 * (n2 + 1)],
                start=False, stop=True), "sPE")
        hbf_free[sl] = y_pe
        slot_gate[e % NSLOT] = y_pe

        W("scalar", "sPE", y_pe)
        if ybf_free[sl]:
            W("scalar", "sW0", ybf_free[sl])
        yb_v = OP("scalar", lambda En, sl=sl: En.activation(
            ybf[sl][:], pY[sl][:], AF.Relu), "sAC")
        pY_free[sl] = yb_v
        W("sync", "sAC", yb_v)
        ystore[e] = OP("sync", lambda En, sl=sl, e=e: En.dma_start(
            out=ypermr[e], in_=ybf[sl][:]), "sW0")
        ybf_free[sl] = ystore[e]

        if e + 8 < E:
            emit_w1_only(e + 8)
        if e + 9 < E:
            emit_w2_stage(e + 9)
        if e + 2 < E:
            emit_b2_load(e + 2, pe_gate=y_pe)

    for e in range(E + 1):
        if e < E:
            p4_front(e)
        if e >= 1:
            p4_back(e - 1)

    out_free = {0: 0, 1: 0}
    ygat_gate = [0] * TC
    st_v = 0

    W("gpsimd", "sW0", ystore[E - 1])
    outr = dout.rearrange("(c p) d -> c p d", p=128)
    xp = [x2pre, x1pre[0]]
    xp_gate = {0: 0, 1: 0}
    pA = {}

    def p5_a(tc):
        sl = tc % 2
        so = 8 * sl
        if tc >= 2:
            W("gpsimd", "sDV", ygat_gate[tc - 2])
        g_v = OP("gpsimd", lambda En, tc=tc, sl=sl: En.indirect_dma_start(
            out=ygat[sl][:], out_offset=None,
            in_=dyperm, in_offset=IOX(destu[:, tc:tc + 1], 0)), "sG")
        W("vector", "sG", g_v)
        if xp_gate[sl]:
            W("vector", "sDV", xp_gate[sl])
        add_v = OP("vector", lambda En, tc=tc, sl=sl: En.tensor_tensor(
            xp[sl][:], x1bf[:, D * tc:D * (tc + 1)], ygat[sl][:], ALU.add), "sDV")
        ygat_gate[tc] = add_v
        W("vector", "sDV", add_v)
        s1 = OP("vector", lambda En, sl=sl, so=so: En.reduce_sum(
            stats[:, so:so + 1], xp[sl][:], axis=AX.X, negate=True), "sDV")
        W("scalar", "sDV", add_v)
        s2 = OP("scalar", lambda En, sl=sl, so=so: En.activation(
            sqscr[:], xp[sl][:], AF.Square, accum_out=stats[:, so + 1:so + 2]), "sAC")
        pA[tc] = (s1, s2)

    def p5_b(tc):
        sl = tc % 2
        so = 8 * sl
        s1, s2 = pA[tc]
        W("vector", "sDV", s1)
        W("vector", "sAC", s2)
        OP("vector", lambda En, so=so: En.tensor_scalar(
            stats[:, so + 2:so + 3], stats[:, so:so + 1], 1.0 / D, None, ALU.mult))
        s3 = OP("vector", lambda En, so=so: En.tensor_scalar(
            stats[:, so + 3:so + 4], stats[:, so + 1:so + 2], 1.0 / D, None, ALU.mult), "sDV")
        W("vector", "sDV", s3)
        s4 = OP("vector", lambda En, so=so: En.tensor_tensor(
            stats[:, so + 4:so + 5], stats[:, so + 2:so + 3], stats[:, so + 2:so + 3], ALU.mult), "sDV")
        W("vector", "sDV", s4)
        s5 = OP("vector", lambda En, so=so: En.tensor_tensor(
            stats[:, so + 5:so + 6], stats[:, so + 3:so + 4], stats[:, so + 4:so + 5], ALU.subtract), "sDV")
        W("scalar", "sDV", s5)
        s6 = OP("scalar", lambda En, so=so: En.activation(
            stats[:, so + 6:so + 7], stats[:, so + 5:so + 6], AF.Sqrt, bias=epst[:]), "sAC")
        W("vector", "sAC", s6)
        s7 = OP("vector", lambda En, so=so: En.reciprocal(
            stats[:, so + 7:so + 8], stats[:, so + 6:so + 7]), "sDV")
        W("vector", "sDV", s7)
        if out_free[sl]:
            W("vector", "sQ", out_free[sl])
        ln_v = OP("vector", lambda En, sl=sl, so=so: En.tensor_scalar(
            outstg[sl][:], xp[sl][:], stats[:, so + 2:so + 3],
            stats[:, so + 7:so + 8], ALU.add, ALU.mult), "sDV")
        xp_gate[sl] = ln_v
        W("sync", "sDV", ln_v)
        stv = OP("sync", lambda En, tc=tc, sl=sl: En.dma_start(
            out=outr[tc], in_=outstg[sl][:]), "sQ")
        out_free[sl] = stv
        return stv

    p5_a(0)
    for tc in range(TC):
        if tc + 1 < TC:
            p5_a(tc + 1)
        st_v = p5_b(tc)

    if DEBUG_DUMPS:
        dba = nc.dram_tensor("dbg_attn", [128, TC * D], BF16, kind="ExternalOutput").ap()
        dbx = nc.dram_tensor("dbg_x1", [128, TC * D], BF16, kind="ExternalOutput").ap()
        dbd = nc.dram_tensor("dbg_destu", [128, TC], U32, kind="ExternalOutput").ap()
        W("sync", "sQ", st_v)
        st_v = OP("sync", lambda En: En.dma_start(out=dba, in_=xfm[:]), "sQ")
        st_v = OP("sync", lambda En: En.dma_start(out=dbx, in_=x1bf[:]), "sQ")
        st_v = OP("sync", lambda En: En.dma_start(out=dbd, in_=destu[:]), "sQ")

    W("sync", "sQ", st_v)

    with nc.Block() as block:
        for eng in ("sync", "scalar", "vector", "tensor", "gpsimd"):
            fns = bd.lists[eng]
            getattr(block, eng)(lambda En, fns=fns: [f(En) for f in fns])

    bd.close()
    al.close()
    return nc


_NC_CACHE = None


def _get_nc():
    global _NC_CACHE
    if _NC_CACHE is None:
        _NC_CACHE = build_nc()
    return _NC_CACHE


def kernel(L0_attn_x, L0_attn_w_qkv, L0_attn_b_qkv, L0_attn_w_out, L0_attn_b_out,
           L0_ffn_router_w, L0_ffn_router_b, L0_ffn_weights1, L0_ffn_bias1,
           L0_ffn_weights2, L0_ffn_bias2):
    x = np.asarray(L0_attn_x, dtype=np.float32)
    wqkv = np.ascontiguousarray(np.asarray(L0_attn_w_qkv, dtype=np.float32))
    bqkv = np.asarray(L0_attn_b_qkv, dtype=np.float32)
    wout = np.ascontiguousarray(np.asarray(L0_attn_w_out, dtype=np.float32))
    bout = np.asarray(L0_attn_b_out, dtype=np.float32)
    wr = np.ascontiguousarray(np.asarray(L0_ffn_router_w, dtype=np.float32))
    br = np.asarray(L0_ffn_router_b, dtype=np.float32)
    w1 = np.ascontiguousarray(np.asarray(L0_ffn_weights1, dtype=np.float32))
    b1 = np.asarray(L0_ffn_bias1, dtype=np.float32)
    w2 = np.ascontiguousarray(np.asarray(L0_ffn_weights2, dtype=np.float32))
    b2 = np.ascontiguousarray(np.asarray(L0_ffn_bias2, dtype=np.float32))

    bqkr = np.ascontiguousarray(bqkv[0:2048][None, :])
    b1c = np.ascontiguousarray(
        np.stack(sum([[b1[e, 0:128], b1[e, 128:256]] for e in range(E)], []), axis=1))
    bv = np.ascontiguousarray(bqkv[2048:3072][None, :])
    bo = np.ascontiguousarray(bout[None, :])
    brr = np.ascontiguousarray(br[None, :])

    identb = np.eye(128, dtype=ml_dtypes.bfloat16)
    ustrb = np.triu(np.ones((128, 128), np.float32), k=1).astype(ml_dtypes.bfloat16)
    ii = np.arange(128)[:, None]
    jj = np.arange(512)[None, :]
    maskb = np.concatenate(
        [np.where(ii + 128 * o < jj, np.float32(NEG_BIG), np.float32(0.0))
         for o in range(4)], axis=1).astype(ml_dtypes.bfloat16)
    iotaf = np.tile(np.arange(E, dtype=np.float32), (128, 1))
    trashf = (E * CAP + 128 * np.arange(TC)[None, :]
              + np.arange(128)[:, None]).astype(np.float32)

    nc = _get_nc()
    common = dict(wqkv=wqkv, wout=wout, wr=wr, w1=w1, w2=w2,
                  bqkr=bqkr, b1c=b1c, bv=bv, bo=bo, br=brr, b2=b2,
                  identb=identb, ustrb=ustrb, maskb=maskb,
                  iotaf=iotaf, trashf=trashf)
    in_maps = [dict(common, x=np.ascontiguousarray(x[b])) for b in range(8)]
    res = run_bass_kernel_spmd(nc, in_maps, core_ids=list(range(8)))
    return np.stack([np.asarray(r["out"], dtype=np.float32) for r in res.results], axis=0)
